# version 25
# baseline (speedup 1.0000x reference)
import math
import numpy as np
import concourse.bass as bass
import concourse.mybir as mybir
from concourse.bass_utils import run_bass_kernel_spmd
from contextlib import ExitStack

F32 = mybir.dt.float32
BF16 = mybir.dt.bfloat16
I32 = mybir.dt.int32
QMAGIC = 0x5F3759DF
AF = mybir.ActivationFunctionType
ALU = mybir.AluOpType
AX = mybir.AxisListType

S = 2048
D = 2048
H = 16
HD = 128
C = 16
T = 16
HID = 8192
EPS1 = 1e-5
EPS2 = 1e-6
SCALE = 1.0 / math.sqrt(HD)
KSTEPS = 4

ENGS = ("t", "sc", "v", "g", "s")


class Ctr:
    def __init__(self, sem):
        self.sem = sem
        self.n = 0


class Prog:

    def __init__(self, ve=None):
        self.ops = {e: [] for e in ENGS}
        self.ve = ve

    def em(self, eng, fn, waits=(), inc=None, k=1, nofence=False):
        waits = list(waits)
        if eng == "v" and self.ve is not None:
            if not nofence:
                waits.append((self.ve, self.ve.n))
            if inc is None:
                inc = self.ve
        w = tuple((c.sem, v) for (c, v) in waits if v > 0)
        self.ops[eng].append((w, fn, None if inc is None else inc.sem, k))
        if inc is not None:
            inc.n += k

    def flush(self, nc, name):
        ops = self.ops
        self.ops = {e: [] for e in ENGS}

        def body(engine, lst):
            seen = {}
            for w, fn, inc_sem, k in lst:
                for sem, val in w:
                    key = id(sem)
                    if seen.get(key, -1) < val:
                        engine.wait_ge(sem, val)
                        seen[key] = val
                inst = fn(engine)
                if inc_sem is not None:
                    inst.then_inc(inc_sem, k)

        with nc.Block(name) as block:
            if ops["t"]:
                @block.tensor
                def _(t):
                    body(t, ops["t"])
            if ops["sc"]:
                @block.scalar
                def _(sc):
                    body(sc, ops["sc"])
            if ops["v"]:
                @block.vector
                def _(v):
                    body(v, ops["v"])
            if ops["g"]:
                @block.gpsimd
                def _(g):
                    body(g, ops["g"])
            if ops["s"]:
                @block.sync
                def _(s):
                    body(s, ops["s"])


class Builder:
    def __init__(self, debug=False):
        self.debug = debug
        self.nc = bass.Bass("TRN2", target_bir_lowering=False, debug=False,
                            detect_race_conditions=False)
        self.P = Prog()
        self.ctx0 = ExitStack()

    def sb(self, st, name, shape, dt):
        return st.enter_context(self.nc.sbuf_tensor(name, list(shape), dt))

    def declare_io(self):
        nc = self.nc
        self.d = {}
        for name, shape in [
            ("x", (S, D)), ("wq_p", (128, H, C, 128)), ("wk_p", (128, H, C, 128)),
            ("wv_p", (128, H, C, 128)), ("o_wT", (D, D)), ("gw_aT", (HD, HD)),
            ("gb_a", (1, HD)), ("identity", (128, 128)), ("ones_row", (1, 128)),
            ("ones_col", (128, 1)), ("nsl", (128, 128)), ("agg_w", (D, D)),
            ("d1_w", (D, HID)), ("d2_w", (HID, D)), ("bc_w", (D, D)),
            ("gwf", (D, D)), ("gbf", (1, D)), ("agg_bT", (128, C)),
            ("d1bT", (128, 64)), ("d2bT", (128, C)), ("bcb", (1, D)),
            ("in_g", (1, D)), ("in_b", (1, D)), ("pn_gT", (128, C)),
            ("pn_bT", (128, C)), ("out_g", (1, D)), ("out_b", (1, D)),
        ]:
            self.d[name] = nc.dram_tensor(name, list(shape), F32,
                                          kind="ExternalInput").ap()
        self.out = nc.dram_tensor("out", [S, D], F32, kind="ExternalOutput").ap()
        if self.debug:
            self.dbg = {}
            for nm, shape, dt in [
                ("dbg_xnfm", [128, C * 2048], BF16),
                ("dbg_bc", [128, D], F32),
                ("dbg_gfm", [128, H * 2048], BF16),
                ("dbg_attn", [S, D], BF16),
                ("dbg_u", [S, D], F32),
            ]:
                self.dbg[nm] = nc.dram_tensor(nm, shape, dt,
                                              kind="ExternalOutput").ap()
        self.attn_sp = nc.dram_tensor("attn_sp", [S, D], BF16, kind="Internal").ap()
        self.u_sp = nc.dram_tensor("u_sp", [S, D], F32, kind="Internal").ap()
        self.xa_sp = nc.dram_tensor("xa_sp", [S, D], F32, kind="Internal").ap()
        self.d1_sp = nc.dram_tensor("d1_sp", [32, 128, 4096], BF16,
                                    kind="Internal").ap()
        self.d2_sp = nc.dram_tensor("d2_sp", [32, 128, 4096], BF16,
                                    kind="Internal").ap()

    def alloc_outer(self):
        nc, c0 = self.nc, self.ctx0
        self.pe = Ctr(c0.enter_context(nc.semaphore("pe")))
        self.ac = Ctr(c0.enter_context(nc.semaphore("ac")))
        self.ve = Ctr(c0.enter_context(nc.semaphore("ve")))
        self.po = Ctr(c0.enter_context(nc.semaphore("po")))
        self.P.ve = self.ve
        def dc(name):
            return Ctr(c0.enter_context(nc.semaphore(name)))
        self.sgc = dc("sgc")
        self.sgc2 = dc("sgc2")
        self.scon1 = dc("scon1")
        self.sconT = dc("sconT")
        self.sconP3 = dc("sconP3")
        self.sxp = [dc("sxp0"), dc("sxp1")]
        self.swb1 = [dc("swb10"), dc("swb11")]
        self.swb2 = [dc("swb20"), dc("swb21")]
        self.ssp = [dc("ssp0"), dc("ssp1")]
        self.sus = [dc("sus0"), dc("sus1")]
        self.sout = dc("sout")
        self.sgw = [dc("sgw0"), dc("sgw1")]
        self.sgt = [dc("sgt0"), dc("sgt1"), dc("sgt2")]
        self.sgu = [dc("sgu0"), dc("sgu1"), dc("sgu2")]
        self.sgb = dc("sgb")
        self.ps_s = c0.enter_context(nc.psum_tensor("ps_s", [128, 2048], F32))
        self.ps_av = c0.enter_context(nc.psum_tensor("ps_av", [128, 512], F32))
        self.ps_g = c0.enter_context(nc.psum_tensor("ps_g", [128, 512], F32))
        self.ps_tr = c0.enter_context(nc.psum_tensor("ps_tr", [128, 512], F32))
        self.ps_m = c0.enter_context(nc.psum_tensor("ps_m", [128, 512], F32))
        self.xnfm = self.sb(c0, "xnfm", [128, C * 2048], BF16)
        self.bcast_rep = self.sb(c0, "bcast_rep", [128, D], F32)
        self.rstd_st = self.sb(c0, "rstd_st", [128, T], F32)
        self.nmr_st = self.sb(c0, "nmr_st", [128, T], F32)
        self.ident_b = self.sb(c0, "ident_b", [128, 128], BF16)
        self.ones_rb = self.sb(c0, "ones_rb", [1, 128], BF16)
        self.ones_cb = self.sb(c0, "ones_cb", [128, 1], BF16)
        self.ones_rf = self.sb(c0, "ones_rf", [1, 128], F32)
        self.ones_cf = self.sb(c0, "ones_cf", [128, 1], F32)
        self.nsl_b = self.sb(c0, "nsl_b", [128, 128], BF16)
        self.gba_b = self.sb(c0, "gba_b", [1, HD], BF16)
        self.gwa_b = self.sb(c0, "gwa_b", [HD, HD], BF16)

    def xnfm_ck(self, c):
        return self.xnfm[:, c * 2048:(c + 1) * 2048]

    def load_consts(self):
        P, d = self.P, self.d
        for dst, src in [(self.ident_b, "identity"), (self.ones_rb, "ones_row"),
                         (self.ones_cb, "ones_col"), (self.nsl_b, "nsl"),
                         (self.gba_b, "gb_a"), (self.gwa_b, "gw_aT")]:
            P.em("g", lambda g, dst=dst, src=src: g.dma_start(out=dst[:], in_=d[src]),
                 inc=self.sgc, k=16)
        for dst, src in [(self.ones_rf, "ones_row"), (self.ones_cf, "ones_col")]:
            P.em("s", lambda s, dst=dst, src=src: s.dma_start(out=dst[:], in_=d[src]),
                 inc=self.scon1, k=16)

    def phase0(self):
        P, d = self.P, self.d
        pe, ac, ve = self.pe, self.ac, self.ve
        ctx = ExitStack()
        self.ctxP0 = ctx
        x_st = self.sb(ctx, "x_st", [128, 4 * D], F32)
        xn_tm = self.sb(ctx, "xn_tm", [128, 2 * D], BF16)
        sq_buf = self.sb(ctx, "sq_buf", [128, 2 * D], F32)
        nmu = self.sb(ctx, "nmu", [128, 2], F32)
        vacc = self.sb(ctx, "vacc", [128, 4], F32)
        sd_b = self.sb(ctx, "sd_b", [128, 2], F32)

        x_done = [0] * T
        ev1_done = [0] * T
        ev2_done = [0] * T
        tr1_done = [0] * T
        tr2_done = [0] * T
        sq_ac = [0] * T
        red_ve = [0] * T
        neg_ve = [0] * T
        nmr_ve = [0] * T
        xld = [0] * T

        def load(tt, first=False):
            buf = (tt % 4) * D
            w = []
            if tt >= 4:
                w.append((ac, x_done[tt - 4]))
            if tt >= 2:
                w.append((ve, red_ve[tt - 2]))
            sxc = self.sxp[tt % 2]
            P.em("s", lambda s, tt=tt, buf=buf: s.dma_start(
                out=x_st[:, buf:buf + D],
                in_=d["x"][tt * 128:(tt + 1) * 128, :]), waits=w, inc=sxc, k=16)
            xld[tt] = sxc.n

        def stats_head(tt):
            buf = (tt % 4) * D
            pp = tt % 2
            sxc = self.sxp[tt % 2]
            wnm_prev = [(ac, sq_ac[tt - 2])] if tt >= 2 else []
            P.em("v", lambda v, buf=buf, pp=pp: v.reduce_sum(
                nmu[:, pp:pp + 1], x_st[:, buf:buf + D], axis=AX.X, negate=True),
                waits=[(sxc, xld[tt])] + wnm_prev, inc=ve)
            red_ve[tt] = ve.n
            P.em("v", lambda v, pp=pp: v.tensor_scalar_mul(
                nmu[:, pp:pp + 1], nmu[:, pp:pp + 1], 1.0 / D), inc=ve)
            neg_ve[tt] = ve.n

        def stats_square(tt):
            buf = (tt % 4) * D
            pp = tt % 2
            P.em("sc", lambda sc, buf=buf, pp=pp: sc.activation(
                sq_buf[:, pp * D:(pp + 1) * D], x_st[:, buf:buf + D], AF.Square,
                bias=nmu[:, pp:pp + 1], accum_out=vacc[:, pp:pp + 1]),
                waits=[(ve, neg_ve[tt])], inc=ac)
            sq_ac[tt] = ac.n

        def stats_mid(tt):
            pp = tt % 2
            P.em("v", lambda v, pp=pp: v.tensor_scalar(
                vacc[:, 2 + pp:3 + pp], vacc[:, pp:pp + 1], 1.0 / D, EPS1,
                ALU.mult, ALU.add), waits=[(ac, sq_ac[tt])], inc=ve)
            vw2 = ve.n
            P.em("sc", lambda sc, pp=pp: sc.activation(
                sd_b[:, pp:pp + 1], vacc[:, 2 + pp:3 + pp], AF.Ln),
                waits=[(ve, vw2)], inc=ac)
            P.em("sc", lambda sc, tt=tt, pp=pp: sc.activation(
                self.rstd_st[:, tt:tt + 1], sd_b[:, pp:pp + 1], AF.Exp,
                scale=-0.5), inc=ac)
            aw2 = ac.n
            P.em("v", lambda v, tt=tt, pp=pp: v.tensor_tensor(
                self.nmr_st[:, tt:tt + 1], nmu[:, pp:pp + 1],
                self.rstd_st[:, tt:tt + 1], ALU.mult),
                waits=[(ac, aw2)], inc=ve)
            nmr_ve[tt] = ve.n

        def stats_id(tt):
            buf = (tt % 4) * D
            pp = tt % 2
            wx = [(ve, nmr_ve[tt])]
            if tt >= 2:
                wx.append((pe, tr2_done[tt - 2]))
            P.em("sc", lambda sc, tt=tt, buf=buf, pp=pp: sc.activation(
                xn_tm[:, pp * D:(pp + 1) * D], x_st[:, buf:buf + D], AF.Identity,
                bias=self.nmr_st[:, tt:tt + 1],
                scale=self.rstd_st[:, tt:tt + 1]), waits=wx, inc=ac)
            x_done[tt] = ac.n

        load(0)
        load(1)
        stats_head(0)
        stats_square(0)
        stats_mid(0)
        stats_id(0)
        for t in range(T):
            pp = t % 2
            for half in range(2):
                wtr = []
                if half == 0:
                    wtr = [(ac, x_done[t]), (self.sgc, self.sgc.n)]
                    if t >= 1:
                        wtr.append((ve, ev1_done[t - 1]))
                elif t >= 1:
                    wtr = [(ve, ev2_done[t - 1])]
                for c in range(half * 8, half * 8 + 8):
                    P.em("t", lambda t_, c=c, pp=pp: t_.matmul(
                        self.ps_s[:, c * 128:(c + 1) * 128],
                        xn_tm[:, pp * D + c * 128:pp * D + (c + 1) * 128],
                        self.ident_b[:], start=True, stop=True),
                        waits=wtr if c == half * 8 else [], inc=pe)
                if half == 0:
                    tr1_done[t] = pe.n
                else:
                    tr2_done[t] = pe.n
            if t + 2 < T:
                load(t + 2)
            if t + 1 < T:
                stats_head(t + 1)
                stats_square(t + 1)

            def evac(t, half):
                trd = tr1_done[t] if half == 0 else tr2_done[t]
                P.em("v", lambda v, t=t, half=half: v.tensor_copy(
                    self.xnfm[:].rearrange("p (c s) -> p c s", c=C)
                    [:, half * 8:half * 8 + 8, t * 128:(t + 1) * 128],
                    self.ps_s[:, half * 1024:(half + 1) * 1024].rearrange(
                        "p (c s) -> p c s", c=8)),
                    waits=[(pe, trd)], inc=ve)
                if half == 0:
                    ev1_done[t] = ve.n
                else:
                    ev2_done[t] = ve.n

            evac(t, 0)
            if t + 1 < T:
                stats_mid(t + 1)
            evac(t, 1)
            if t + 1 < T:
                stats_id(t + 1)

    def thought_alloc(self):
        ctxP = ExitStack()
        self.ctxTP = ctxP
        self.thc = self.sb(ctxP, "thc", [128, 3 * 4096], BF16)
        self.rowstash = self.sb(ctxP, "rowstash", [1, 2 * 256], F32)
        self.png_t = self.sb(ctxP, "png_t", [128, C], F32)
        self.pnb_t = self.sb(ctxP, "pnb_t", [128, C], F32)
        self.aggb_t = self.sb(ctxP, "aggb_t", [128, C], F32)
        self.d2b_t = self.sb(ctxP, "d2b_t", [128, C], F32)
        self.d1b_t = self.sb(ctxP, "d1b_t", [128, 64], F32)
        self.tho = self.sb(ctxP, "tho", [128, C], F32)
        self.tho_bf = self.sb(ctxP, "tho_bf", [128, C], BF16)
        self.t_bf = self.sb(ctxP, "t_bf", [128, C], BF16)
        self.tact = self.sb(ctxP, "tact", [128, 64], BF16)
        self.tmp16 = self.sb(ctxP, "tmp16", [128, 64], F32)
        self.sc1 = self.sb(ctxP, "sc1", [1, 8], F32)
        self.ttf = self.sb(ctxP, "ttf", [128, 64], F32)
        self.nm_b = self.sb(ctxP, "nm_b", [128, 1], F32)
        self.rs_b = self.sb(ctxP, "rs_b", [128, 1], F32)

    def thought_prologue(self):
        P, d = self.P, self.d
        pe, ac, ve = self.pe, self.ac, self.ve
        sconT, sgc2 = self.sconT, self.sgc2
        sgu = self.sgu
        png_t, pnb_t, aggb_t = self.png_t, self.pnb_t, self.aggb_t
        d2b_t, d1b_t = self.d2b_t, self.d1b_t
        thc = self.thc
        ctx = ExitStack()
        self.ctxT = ctx
        xl = self.sb(ctx, "xl", [1, D], F32)
        xl2 = self.sb(ctx, "xl2", [1, D], F32)
        xl_bf = self.sb(ctx, "xl_bf", [1, D], BF16)
        ing_r = self.sb(ctx, "ing_r", [1, D], F32)
        inb_r = self.sb(ctx, "inb_r", [1, D], F32)
        xnlT = self.sb(ctx, "xnlT", [128, C], BF16)

        P.em("s", lambda s: s.dma_start(out=xl[:], in_=d["x"][S - 1:S, :]),
             inc=sconT, k=16)
        for dst, src in [(ing_r, "in_g"), (inb_r, "in_b"), (png_t, "pn_gT"),
                         (pnb_t, "pn_bT"), (aggb_t, "agg_bT"), (d2b_t, "d2bT"),
                         (d1b_t, "d1bT")]:
            P.em("s", lambda s, dst=dst, src=src: s.dma_start(out=dst[:], in_=d[src]),
                 inc=sconT, k=16)
        trow_sx = sconT.n
        self.trow_sx = trow_sx

        ps_m = self.ps_m
        sc1 = self.sc1

        def row_ln(src, dst, eps):
            P.em("v", lambda v: v.reduce_sum(sc1[:, 0:1], src[:], axis=AX.X,
                                             negate=True), waits=[(ac, ac.n)])
            P.em("v", lambda v: v.tensor_scalar_mul(sc1[:, 0:1], sc1[:, 0:1], 1.0 / D))
            w1 = ve.n
            P.em("sc", lambda sc: sc.activation(dst[:], src[:], AF.Square,
                                                bias=sc1[:, 0:1],
                                                accum_out=sc1[:, 1:2]),
                 waits=[(ve, w1)], inc=ac)
            w2 = ac.n
            P.em("v", lambda v: v.tensor_scalar(sc1[:, 1:2], sc1[:, 1:2],
                                                1.0 / D, eps, ALU.mult, ALU.add),
                 waits=[(ac, w2)])
            w3 = ve.n
            P.em("sc", lambda sc: sc.activation(sc1[:, 2:3], sc1[:, 1:2], AF.Ln),
                 waits=[(ve, w3)], inc=ac)
            P.em("sc", lambda sc: sc.activation(sc1[:, 3:4], sc1[:, 2:3], AF.Exp,
                                                scale=-0.5), inc=ac)
            w4 = ac.n
            P.em("v", lambda v: v.tensor_tensor(sc1[:, 4:5], sc1[:, 0:1],
                                                sc1[:, 3:4], ALU.mult),
                 waits=[(ac, w4)])
            w5 = ve.n
            P.em("sc", lambda sc: sc.activation(dst[:], src[:], AF.Identity,
                                                bias=sc1[:, 4:5], scale=sc1[:, 3:4]),
                 waits=[(ve, w5)], inc=ac)

        P.em("v", lambda v: v.tensor_copy(xl2[:], xl[:]), waits=[(sconT, trow_sx)])
        row_ln(xl, xl2, EPS1)
        row_ln(xl2, xl, EPS2)
        P.em("v", lambda v: v.tensor_tensor(xl[:], xl[:], ing_r[:], ALU.mult),
             waits=[(ac, ac.n)])
        P.em("v", lambda v: v.tensor_tensor(xl_bf[:], xl[:], inb_r[:], ALU.add))
        wxl = ve.n
        for c in range(C):
            P.em("t", lambda t_, c=c: t_.matmul(
                ps_m[:, 96 + c:97 + c], xl_bf[0:1, c * 128:(c + 1) * 128],
                self.ones_rb[0:1, 0:1], start=True, stop=True),
                waits=[(ve, wxl), (self.sgc, self.sgc.n),
                       (self.scon1, self.scon1.n)] if c == 0 else [], inc=pe)
        wp = pe.n
        P.em("v", lambda v: v.tensor_copy(xnlT[:], ps_m[:, 96:112]),
             waits=[(pe, wp)])
        wxnl = ve.n

        self.th_slot = 0
        self.th_slot_pe = [0, 0, 0]
        self.th_slot_wb = [0, 0, 0]
        self.th_last_ld_sem = [None, None, None]
        self.th_ev = [0]
        self.th_cp = [0]
        self.th_g = 0
        self.pend_close = None
        self.wb_sems = [self.swb1[0], self.swb1[1], self.swb2[0]]

        def unit_load(src_ap, n_el, wb_dst=None, sync_q=False):
            sl = self.th_slot % 3
            self.th_slot += 1
            sgp = (self.sgu if sync_q else self.sgt)[sl]
            ww = []
            if self.th_slot_pe[sl]:
                ww.append((pe, self.th_slot_pe[sl]))
            if self.th_slot_wb[sl]:
                ww.append((self.wb_sems[sl], self.th_slot_wb[sl]))
            eng = "s" if sync_q else "g"
            P.em(eng, lambda e, src_ap=src_ap, sl=sl, n_el=n_el: e.dma_start(
                out=self.thc[:, sl * 4096:sl * 4096 + n_el].rearrange(
                    "p (c n) -> p c n", c=src_ap.shape[1]) if len(src_ap.shape) == 3
                else self.thc[:, sl * 4096:sl * 4096 + n_el], in_=src_ap),
                waits=ww, inc=sgp, k=16)
            wld = sgp.n
            if wb_dst is not None:
                P.em("s", lambda s, wb_dst=wb_dst, sl=sl, n_el=n_el: s.dma_start(
                    out=wb_dst, in_=self.thc[:, sl * 4096:sl * 4096 + n_el]),
                    waits=[(sgp, wld)], inc=self.wb_sems[sl], k=16)
                self.th_slot_wb[sl] = self.wb_sems[sl].n
            self.th_last_ld_sem[sl] = sgp
            return sl, wld

        def row_mms(sl, wld, lhs_col_fn, blk0, nblk, start, stop, first_w=()):
            if start:
                flush_close()
            sgp = self.th_last_ld_sem[sl]
            for b in range(nblk):
                ww = []
                if b == 0:
                    ww = [(sgp, wld)] + list(first_w)
                    if start:
                        ww.append((ac, self.th_ev[0]))
                        ww.append((ve, self.th_cp[0]))
                P.em("t", lambda t_, sl=sl, b=b, blk0=blk0, start=start,
                     stop=stop, nblk=nblk: t_.matmul(
                    ps_m[0:1, 0:512],
                    lhs_col_fn(blk0 + b),
                    self.thc[:, sl * 4096 + b * 512:sl * 4096 + (b + 1) * 512],
                    start=(start and b == 0), stop=(stop and b == nblk - 1)),
                    waits=ww, inc=pe)

        def row_close(dst_cols):
            wg = pe.n
            P.em("sc", lambda sc: sc.copy(self.rowstash[0:1, 0:512],
                                          ps_m[0:1, 0:512]),
                 waits=[(pe, wg)], inc=ac)
            self.th_ev[0] = ac.n
            self.pend_close = (dst_cols, ac.n)
            flush_close()

        def flush_close():
            if self.pend_close is None:
                return
            dst_cols, wev = self.pend_close
            self.pend_close = None
            for hh in range(4):
                P.em("t", lambda t_, hh=hh: t_.matmul(
                    ps_m[:, hh:hh + 1],
                    self.rowstash[0:1, hh * 128:(hh + 1) * 128],
                    self.ones_cf[0:1, 0:1], start=True, stop=True),
                    waits=[(ac, wev)] if hh == 0 else [], inc=pe)
            wtr = pe.n
            P.em("v", lambda v, dst_cols=dst_cols: v.tensor_copy(
                self.ttf[:, dst_cols:dst_cols + 4], ps_m[:, 0:4]),
                waits=[(pe, wtr)], inc=ve, nofence=True)
            self.th_cp[0] = ve.n

        self.unit_load = unit_load
        self.row_mms = row_mms
        self.row_close = row_close
        self.flush_close = flush_close

        for ng in range(4):
            srcA = d["agg_w"][0:1024, ng * 512:(ng + 1) * 512].rearrange(
                "(c p) n -> p c n", p=128)
            srcB = d["agg_w"][1024:2048, ng * 512:(ng + 1) * 512].rearrange(
                "(c p) n -> p c n", p=128)
            slA, wldA = unit_load(srcA, 4096)
            slB, wldB = unit_load(srcB, 4096)
            fw = [(ve, wxnl)] if ng == 0 else ()
            row_mms(slA, wldA, lambda b: xnlT[:, b:b + 1], 0, 8, True, False,
                    first_w=fw)
            row_mms(slB, wldB, lambda b: xnlT[:, b:b + 1], 8, 8, False, True)
            row_close(ng * 4)
            self.th_slot_pe[slA] = pe.n
            self.th_slot_pe[slB] = pe.n
        flush_close()
        P.em("v", lambda v: v.tensor_tensor(self.tho[:], self.ttf[:, 0:16],
                                            aggb_t[:], ALU.add),
             waits=[(sconT, trow_sx)])
        self._th_gen = self._thought_gen()

    def _thought_gen(self):
        P, d = self.P, self.d
        pe, ac, ve = self.pe, self.ac, self.ve
        ps_m = self.ps_m
        sc1, nm_b, rs_b = self.sc1, self.nm_b, self.rs_b
        tho, tho_bf, t_bf, tact, tmp16 = (self.tho, self.tho_bf, self.t_bf,
                                          self.tact, self.tmp16)

        for step in range(KSTEPS):
            P.em("v", lambda v: v.reduce_sum(nm_b[:], tho[:], axis=AX.X,
                                             negate=True))
            w1 = ve.n
            P.em("t", lambda t_: t_.matmul(ps_m[0:1, 112:113], self.ones_cf[:],
                                           nm_b[:], start=True, stop=True),
                 waits=[(ve, w1)], inc=pe)
            w2 = pe.n
            P.em("v", lambda v: v.tensor_scalar_mul(sc1[:, 0:1], ps_m[0:1, 112:113],
                                                    1.0 / D), waits=[(pe, w2)])
            w3 = ve.n
            P.em("t", lambda t_: t_.matmul(ps_m[:, 113:114], self.ones_rf[0:1, :],
                                           sc1[0:1, 0:1], start=True, stop=True),
                 waits=[(ve, w3)], inc=pe)
            w4 = pe.n
            P.em("v", lambda v: v.tensor_copy(nm_b[:], ps_m[:, 113:114]),
                 waits=[(pe, w4)])
            w5 = ve.n
            P.em("sc", lambda sc: sc.activation(tmp16[:, 0:C], tho[:], AF.Square,
                                                bias=nm_b[:], accum_out=rs_b[:]),
                 waits=[(ve, w5)], inc=ac)
            w6 = ac.n
            P.em("t", lambda t_: t_.matmul(ps_m[0:1, 112:113], self.ones_cf[:],
                                           rs_b[:], start=True, stop=True),
                 waits=[(ac, w6)], inc=pe)
            w7 = pe.n
            P.em("v", lambda v: v.tensor_scalar(sc1[:, 1:2], ps_m[0:1, 112:113],
                                                1.0 / D, EPS2, ALU.mult, ALU.add),
                 waits=[(pe, w7)])
            w8 = ve.n
            P.em("sc", lambda sc: sc.activation(sc1[:, 2:3], sc1[:, 1:2], AF.Ln),
                 waits=[(ve, w8)], inc=ac)
            P.em("sc", lambda sc: sc.activation(sc1[:, 3:4], sc1[:, 2:3], AF.Exp,
                                                scale=-0.5), inc=ac)
            w9 = ac.n
            P.em("v", lambda v: v.nop(), waits=[(ac, w9)])
            w10 = ve.n
            P.em("t", lambda t_: t_.matmul(ps_m[:, 113:114], self.ones_rf[0:1, :],
                                           sc1[0:1, 3:4], start=True, stop=True),
                 waits=[(ve, w10)], inc=pe)
            w11 = pe.n
            P.em("v", lambda v: v.tensor_copy(rs_b[:], ps_m[:, 113:114]),
                 waits=[(pe, w11)])
            P.em("v", lambda v: v.tensor_scalar(tmp16[:, 0:C], tho[:], nm_b[:],
                                                rs_b[:], ALU.add, ALU.mult))
            P.em("v", lambda v: v.tensor_tensor(tmp16[:, 0:C], tmp16[:, 0:C],
                                                self.png_t[:], ALU.mult))
            P.em("v", lambda v: v.tensor_tensor(t_bf[:], tmp16[:, 0:C],
                                                self.pnb_t[:], ALU.add))
            wtn = ve.n
            yield

            if step == 0:
                self.d1_wb_marks = [None] * 32
            for ng in range(16):
                pair = []
                for hf in range(2):
                    idx = ng * 2 + hf
                    if step == 0:
                        sap = d["d1_w"][hf * 1024:(hf + 1) * 1024,
                                        ng * 512:(ng + 1) * 512].rearrange(
                            "(c p) n -> p c n", p=128)
                        sl, wld = self.unit_load(sap, 4096,
                                                 wb_dst=self.d1_sp[idx])
                        self.d1_wb_marks[idx] = (self.wb_sems[sl],
                                                 self.wb_sems[sl].n)
                    else:
                        sl, wld = self.unit_load(self.d1_sp[idx], 4096,
                                                 sync_q=True)
                        if step == 1:
                            wbs, wbn = self.d1_wb_marks[idx]
                            P.ops["s"][-1] = (P.ops["s"][-1][0]
                                              + ((wbs.sem, wbn),),) + \
                                P.ops["s"][-1][1:]
                    pair.append((sl, wld))
                    yield
                (slA, wldA), (slB, wldB) = pair
                fw = [(ve, wtn)] if ng == 0 else ()
                self.row_mms(slA, wldA, lambda b: t_bf[:, b:b + 1], 0, 8,
                             True, False, first_w=fw)
                self.row_mms(slB, wldB, lambda b: t_bf[:, b:b + 1], 8, 8,
                             False, True)
                self.row_close(ng * 4)
                self.th_slot_pe[slA] = pe.n
                self.th_slot_pe[slB] = pe.n
            self.flush_close()
            P.em("v", lambda v: v.tensor_tensor(self.ttf[:], self.ttf[:],
                                                self.d1b_t[:], ALU.add),
                 waits=[(self.sconT, self.trow_sx)])
            wb1 = ve.n
            P.em("sc", lambda sc: sc.activation(tact[:], self.ttf[:], AF.Gelu),
                 waits=[(ve, wb1)], inc=ac)
            wge = ac.n
            yield

            upd_wait = ve.n
            if step == 0:
                self.d2_wb_marks = [None] * 32
            for ng2 in range(4):
                for q in range(8):
                    idx = ng2 * 8 + q
                    if step == 0:
                        sap = d["d2_w"][q * 1024:(q + 1) * 1024,
                                        ng2 * 512:(ng2 + 1) * 512].rearrange(
                            "(c p) n -> p c n", p=128)
                        sl, wld = self.unit_load(sap, 4096,
                                                 wb_dst=self.d2_sp[idx])
                        self.d2_wb_marks[idx] = (self.wb_sems[sl],
                                                 self.wb_sems[sl].n)
                    else:
                        sl, wld = self.unit_load(self.d2_sp[idx], 4096,
                                                 sync_q=True)
                        if step == 1:
                            wbs, wbn = self.d2_wb_marks[idx]
                            P.ops["s"][-1] = (P.ops["s"][-1][0]
                                              + ((wbs.sem, wbn),),) + \
                                P.ops["s"][-1][1:]
                    fw = []
                    if ng2 == 0 and q == 0:
                        fw = [(ac, wge), (ve, upd_wait)]
                    self.row_mms(sl, wld, lambda b: tact[:, b:b + 1], q * 8, 8,
                                 q == 0, q == 7, first_w=fw)
                    if q == 7:
                        self.row_close(16 + ng2 * 4)
                    self.th_slot_pe[sl] = pe.n
                    yield
            self.flush_close()
            P.em("v", lambda v: v.tensor_tensor(tmp16[:, 0:C],
                                                self.ttf[:, 16:32],
                                                self.d2b_t[:], ALU.add))
            P.em("v", lambda v: v.tensor_tensor(tho[:], tho[:], tmp16[:, 0:C],
                                                ALU.add))
            yield
        P.em("v", lambda v: v.tensor_copy(tho_bf[:], tho[:]))
        self.wtb = ve.n

    def thought_bc(self, ps_free_waits, bcb_bf, bcrow):
        P, d = self.P, self.d
        pe, ac, ve = self.pe, self.ac, self.ve
        ps_m = self.ps_m
        P.em("g", lambda g, bcb_bf=bcb_bf: g.dma_start(out=bcb_bf, in_=d["bcb"]),
             inc=self.sgc2, k=16)
        wbcb = self.sgc2.n
        first = [(ve, self.wtb), (self.sgc2, wbcb)] + list(ps_free_waits)
        for ng in range(4):
            srcA = d["bc_w"][0:1024, ng * 512:(ng + 1) * 512].rearrange(
                "(c p) n -> p c n", p=128)
            srcB = d["bc_w"][1024:2048, ng * 512:(ng + 1) * 512].rearrange(
                "(c p) n -> p c n", p=128)
            slA, wldA = self.unit_load(srcA, 4096)
            slB, wldB = self.unit_load(srcB, 4096)
            fw = first if ng == 0 else ()
            self.row_mms(slA, wldA, lambda b: self.tho_bf[:, b:b + 1],
                         0, 8, True, False, first_w=fw)
            self.row_mms(slB, wldB, lambda b: self.tho_bf[:, b:b + 1],
                         8, 8, False, False)
            off = ng * 512
            P.em("t", lambda t_, off=off, bcb_bf=bcb_bf: t_.matmul(
                ps_m[0:1, 0:512], self.ones_rb[0:1, 0:1],
                bcb_bf[0:1, off:off + 512], start=False, stop=True), inc=pe)
            wg = pe.n
            P.em("sc", lambda sc, off=off, bcrow=bcrow: sc.copy(
                bcrow[0:1, off:off + 512], ps_m[0:1, 0:512]),
                waits=[(pe, wg)], inc=ac)
            self.th_ev[0] = ac.n
            self.th_slot_pe[slA] = pe.n
            self.th_slot_pe[slB] = pe.n
        wrow = ac.n
        for gi in range(4):
            P.em("t", lambda t_, gi=gi, bcrow=bcrow: t_.matmul(
                self.ps_s[:, gi * 512:(gi + 1) * 512], self.ones_rf[0:1, :],
                bcrow[0:1, gi * 512:(gi + 1) * 512], start=True, stop=True),
                waits=[(ac, wrow)] if gi == 0 else [], inc=pe)
        wrep = pe.n
        P.em("v", lambda v: v.tensor_copy(self.bcast_rep[:], self.ps_s[:, 0:2048]),
             waits=[(pe, wrep)])
        self.bcast_done_ve = self.ve.n

    def attention(self):
        P, d = self.P, self.d
        pe, ac, ve = self.pe, self.ac, self.ve
        sgw = self.sgw
        ctxAB = ExitStack()
        self.ctxAB = ctxAB
        self.gatedfm = self.sb(ctxAB, "gatedfm", [128, H * 2048], BF16)
        ctx = ExitStack()
        self.ctxP1 = ctx
        wq_s = self.sb(ctx, "wq_s", [128, 2 * 2048], BF16)
        wk_s = self.sb(ctx, "wk_s", [128, 2 * 2048], BF16)
        wv_s = self.sb(ctx, "wv_s", [128, 2 * 2048], BF16)
        q_fm = self.sb(ctx, "q_fm", [128, 2048], BF16)
        k_fm = self.sb(ctx, "k_fm", [128, 2048], BF16)

        v_aug = self.sb(ctx, "v_aug", [128, T * 132], BF16)
        p_t = self.sb(ctx, "p_t", [128, 2048], BF16)
        v_fm = p_t
        th_sb = self.sb(ctx, "th_sb", [128, 256], F32)
        th1_sb = self.sb(ctx, "th1_sb", [128, 128], F32)
        gated_bf = self.sb(ctx, "gated_bf", [128, 128], BF16)

        nmx = self.sb(ctx, "nmx", [1, 4], F32)
        negm_b = self.sb(ctx, "negm_b", [128, 1], F32)
        rs2 = self.sb(ctx, "rs2", [128, 1], F32)
        recip = self.sb(ctx, "recip", [128, 1], F32)

        start_w = [(pe, self.pre_attn_pe), (ve, self.pre_attn_ve),
                   (ac, self.pre_attn_ac),
                   (self.swb1[0], self.swb1[0].n), (self.swb1[1], self.swb1[1].n),
                   (self.swb2[0], self.swb2[0].n), (self.swb2[1], self.swb2[1].n)]
        P.em("v", lambda v: v.memset(
            v_aug[:].rearrange("p (j w) -> p j w", w=132)[:, :, 128:129], 1.0),
            waits=start_w, inc=ve)
        vones = ve.n

        wslot_pe = [0, 0]
        exp_done = 0
        stt_done = [0] * T
        tr_done = [0] * T
        trevac_done = [0] * T

        for h in range(H):
            buf = (h % 2) * 2048
            sgp = sgw[h % 2]
            ww = [(pe, wslot_pe[h % 2])] if h >= 2 else list(start_w)
            for wdst, wsrc in [(wq_s, "wq_p"), (wk_s, "wk_p"), (wv_s, "wv_p")]:
                P.em("g", lambda g, wdst=wdst, wsrc=wsrc, h=h, buf=buf: g.dma_start(
                    out=wdst[:, buf:buf + 2048],
                    in_=d[wsrc][:, h].rearrange("p c e -> p (c e)")),
                    waits=ww, inc=sgp, k=16)
                ww = []
            wld = sgp.n

            def proj(wsl, dst, first_wait):
                halves = []
                for gh in range(2):
                    for gi in range(2):
                        g0 = gh * 1024 + gi * 512
                        for c in range(C):
                            ww2 = []
                            if gh == 0 and gi == 0 and c == 0:
                                ww2 = list(first_wait)
                            P.em("t", lambda t_, c=c, g0=g0, wsl=wsl, buf=buf: t_.matmul(
                                self.ps_s[:, g0:g0 + 512],
                                wsl[:, buf + c * 128:buf + (c + 1) * 128],
                                self.xnfm[:, c * 2048 + g0:c * 2048 + g0 + 512],
                                start=(c == 0), stop=(c == C - 1)),
                                waits=ww2, inc=pe)
                    wg = pe.n
                    P.em("v", lambda v, gh=gh, dst=dst: v.tensor_copy(
                        dst[:, gh * 1024:(gh + 1) * 1024],
                        self.ps_s[:, gh * 1024:(gh + 1) * 1024]),
                        waits=[(pe, wg)], inc=ve)
                    halves.append(ve.n)
                return halves

            wq0 = [(sgp, wld), (ac, exp_done)] if h > 0 else \
                  [(sgp, wld), (ve, self.pre_attn_ve)]
            qh = proj(wq_s, q_fm, wq0)
            kh = proj(wk_s, k_fm, [(ve, qh[1])])
            vh = proj(wv_s, v_fm, [(ve, kh[1])])

            for j in range(T):
                ww3 = [(ve, vh[j // 8])]
                if j < 2:
                    if j == 0:
                        ww3.append((ve, vones))
                    if h > 0:
                        ww3.append((ve, self.trevac_prev))
                        ww3.append((ve, self.stt_prev_head))
                if j >= 2:
                    ww3.append((ve, vtr_ev[j - 2]))
                vps = self.ps_tr if (j % 2 == 0) else self.ps_av
                P.em("t", lambda t_, j=j, vps=vps: t_.matmul(
                    vps[:, 0:128],
                    v_fm[:, j * 128:(j + 1) * 128], self.ident_b[:],
                    start=True, stop=True), waits=ww3, inc=pe)
                wv1 = pe.n
                P.em("v", lambda v, j=j, vps=vps: v.tensor_copy(
                    v_aug[:, j * 132:j * 132 + 128], vps[:, 0:128]),
                    waits=[(pe, wv1)], inc=ve)
                if j == 0:
                    vtr_ev = [0] * T
                vtr_ev[j] = ve.n

            for idx, (src_fm, done) in enumerate([(q_fm, qh[1]), (k_fm, kh[1])]):
                P.em("v", lambda v, src_fm=src_fm: v.tensor_tensor(
                    p_t[:], src_fm[:], src_fm[:], ALU.mult),
                    waits=[(ve, done), (pe, nrm_pe[idx] if False else 0)], inc=ve)
                wq2 = ve.n
                for gi in range(4):
                    ww = []
                    if gi == 0:
                        ww = [(ve, wq2), (ve, vh[1])]
                        if idx == 1:
                            ww.append((ve, red_q))
                    P.em("t", lambda t_, gi=gi: t_.matmul(
                        self.ps_s[0:1, gi * 512:(gi + 1) * 512], self.ones_cb[:],
                        p_t[:, gi * 512:(gi + 1) * 512], start=True, stop=True),
                        waits=ww, inc=pe)
                wmm_ = pe.n
                P.em("v", lambda v, idx=idx: v.reduce_max(
                    nmx[:, idx:idx + 1], self.ps_s[0:1, 0:2048], axis=AX.X),
                    waits=[(pe, wmm_)], inc=ve)
                red_q = ve.n
            P.em("v", lambda v: v.tensor_tensor(nmx[:, 2:3], nmx[:, 0:1],
                                                nmx[:, 1:2], ALU.add), inc=ve)
            P.em("v", lambda v: v.tensor_scalar_mul(nmx[:, 2:3], nmx[:, 2:3],
                                                    -0.5 * SCALE), inc=ve)
            wnm2 = ve.n
            P.em("t", lambda t_: t_.matmul(self.ps_g[:, 2:3], self.ones_rf[0:1, :],
                                           nmx[0:1, 2:3], start=True, stop=True),
                 waits=[(ve, wnm2), (ac, tanh_prev if h > 0 else 0)], inc=pe)
            wnm3 = pe.n
            P.em("v", lambda v: v.tensor_copy(negm_b[:], self.ps_g[:, 2:3]),
                 waits=[(pe, wnm3)], inc=ve)
            wnegm = ve.n

            st_done = [0] * T
            stc_done = [0, 0]
            expc_done = [exp_done, exp_done]
            gate_done = [0] * T
            av_done = [0] * T
            exp_list = [0] * T
            exp_h1 = [0] * T
            tanh_list = [0] * T
            th1_list = [0] * T

            for m in range(T + 2):
                if 1 <= m <= T:
                    mm = m - 1
                    for j in range(mm + 1):
                        ww4 = []
                        if j == 0:
                            ww4 = [(ac, exp_h1[mm])]
                            if mm == 0:
                                ww4.append((ve, vtr_ev[T - 1]))
                            if mm >= 1:
                                ww4.append((ve, stt_done[mm - 1]))
                            elif h > 0:
                                ww4.append((ve, self.stt_prev_head))
                        elif j == 8:
                            ww4 = [(ac, exp_list[mm])]
                        P.em("t", lambda t_, j=j, mm=mm: t_.matmul(
                            self.ps_av[:, 0:129],
                            p_t[:, j * 128:(j + 1) * 128],
                            v_aug[:, j * 132:j * 132 + 129],
                            start=(j == 0), stop=(j == mm)),
                            waits=ww4, inc=pe)
                    av_done[mm] = pe.n
                if m < T:
                    for half_ in range(2):
                        j0, j1 = (0, min(8, m + 1)) if half_ == 0 else (8, m + 1)
                        if j0 >= j1:
                            continue
                        for j in range(j0, j1):
                            ww5 = []
                            if j == j0:
                                ww5 = [(ac, expc_done[half_]), (ve, wnegm)]
                                if m == 0:
                                    ww5.append((ve, vtr_ev[T - 1]))
                            diag = (j == m)
                            P.em("t", lambda t_, j=j, m=m, diag=diag: t_.matmul(
                                self.ps_s[:, j * 128:(j + 1) * 128],
                                k_fm[:, j * 128:(j + 1) * 128],
                                q_fm[:, m * 128:(m + 1) * 128],
                                start=True, stop=not diag), waits=ww5, inc=pe)
                            if diag:
                                P.em("t", lambda t_, j=j: t_.matmul(
                                    self.ps_s[:, j * 128:(j + 1) * 128],
                                    self.nsl_b[:], self.ident_b[:],
                                    start=False, stop=True), inc=pe)
                        stc_done[half_] = pe.n
                        P.em("sc", lambda sc, m=m, j0=j0, j1=j1: sc.activation(
                            p_t[:, j0 * 128:j1 * 128],
                            self.ps_s[:, j0 * 128:j1 * 128],
                            AF.Exp, bias=negm_b[:], scale=SCALE),
                            waits=[(pe, stc_done[half_])], inc=ac)
                        expc_done[half_] = ac.n
                        if half_ == 0:
                            exp_h1[m] = ac.n
                    st_done[m] = pe.n
                    exp_list[m] = ac.n
                    exp_done = ac.n
                    ww6 = [(ac, tanh_list[m - 1] if m > 0 else
                            (tanh_prev if h > 0 else 0))]
                    if m == 0:
                        ww6.append((ve, wnegm))
                    P.em("t", lambda t_, m=m: t_.matmul(
                        self.ps_g[:, 0:128], q_fm[:, m * 128:(m + 1) * 128],
                        self.gwa_b[:], start=True, stop=False), waits=ww6, inc=pe)
                    P.em("t", lambda t_: t_.matmul(
                        self.ps_g[:, 0:128], self.ones_rb[0:1, :], self.gba_b[0:1, :],
                        start=False, stop=True), inc=pe)
                    gate_done[m] = pe.n
                if 2 <= m:
                    mm2 = m - 2
                    ww7 = [(ve, stt_done[mm2])]
                    if mm2 >= 1:
                        ww7.append((ve, trevac_done[mm2 - 1]))
                    else:
                        ww7.append((ve, vtr_ev[T - 2]))
                        if h > 0:
                            ww7.append((ve, self.trevac_prev))
                    P.em("t", lambda t_, mm2=mm2: t_.matmul(
                        self.ps_tr[:, 0:128],
                        gated_bf[:], self.ident_b[:], start=True, stop=True),
                        waits=ww7, inc=pe)
                    tr_done[mm2] = pe.n
                if m < T:
                    ww8 = [(pe, gate_done[m])]
                    if m >= 2:
                        ww8.append((ve, th1_list[m - 2]))
                    P.em("sc", lambda sc, m=m: sc.activation(
                        th_sb[:, (m % 2) * 128:(m % 2) * 128 + 128],
                        self.ps_g[:, 0:128], AF.Tanh, scale=0.5), waits=ww8, inc=ac)
                    tanh_list[m] = ac.n

                if 1 <= m <= T:
                    mm = m - 1
                    P.em("v", lambda v: v.reciprocal(recip[:], self.ps_av[:, 128:129]),
                         waits=[(pe, av_done[mm])], inc=ve)
                    P.em("v", lambda v, mm=mm: v.tensor_scalar(
                        th1_sb[:], th_sb[:, (mm % 2) * 128:(mm % 2) * 128 + 128],
                        0.5, 0.5, ALU.mult, ALU.add),
                        waits=[(ac, tanh_list[mm])], inc=ve)
                    th1_list[mm] = ve.n
                    ww9 = [(pe, tr_done[mm - 1] if mm >= 1 else 0)]
                    if mm == 0 and h > 0:
                        ww9.append((pe, self.tr_prev_head))
                    P.em("v", lambda v: v.scalar_tensor_tensor(
                        gated_bf[:], self.ps_av[:, 0:128], recip[:], th1_sb[:],
                        ALU.mult, ALU.mult), waits=ww9, inc=ve)
                    stt_done[mm] = ve.n
                if 2 <= m:
                    mm2 = m - 2
                    P.em("v", lambda v, mm2=mm2, h=h: v.tensor_copy(
                        self.gatedfm[:, h * 2048 + mm2 * 128:h * 2048 + (mm2 + 1) * 128],
                        self.ps_tr[:, 0:128]),
                        waits=[(pe, tr_done[mm2])], inc=ve, nofence=True)
                    trevac_done[mm2] = ve.n
                next(self._th_gen, None)

            tanh_prev = tanh_list[T - 1]
            self.stt_prev_head = stt_done[T - 1]
            self.tr_prev_head = tr_done[T - 1]
            self.trevac_prev = trevac_done[T - 1]
            wslot_pe[h % 2] = pe.n
        self.p1_pe, self.p1_ve, self.p1_ac = pe.n, ve.n, ac.n
        self.exp_done_final = exp_done

    def oproj(self):
        P, d = self.P, self.d
        pe, ac, ve = self.pe, self.ac, self.ve
        sgb, ssp = self.sgb, self.ssp
        self.ctxP1.close()
        ctx = ExitStack()
        self.ctxP2 = ctx
        o_half = self.sb(ctx, "o_half", [128, C * 1024], BF16)
        bounce = self.sb(ctx, "bounce", [128, 2 * 1024], BF16)
        obeta_bf = self.sb(ctx, "obeta_bf", [1, D], BF16)

        for _ in self._th_gen:
            pass
        self.thought_bc([(ac, self.p1_ac), (ve, self.p1_ve)],
                        bounce[0:1, 0:2048], self.bcast_rep[0:1, 0:2048])
        start_w = [(pe, self.p1_pe), (ve, self.p1_ve), (ac, self.p1_ac)]
        if self.debug:
            P.em("s", lambda s: s.dma_start(out=self.dbg["dbg_gfm"],
                                            in_=self.gatedfm[:]),
                 waits=list(start_w), inc=self.sout, k=16)
        ev_done = [0, 0]
        sp_done = [0, 0]
        P.em("g", lambda g: g.dma_start(out=obeta_bf[:], in_=d["out_b"]),
             waits=list(start_w), inc=sgb, k=16)
        for half in range(2):
            wld_list = []
            for c in range(C):
                ww = []
                if half == 1 and c == 0:
                    ww = [(pe, pe.n)]
                P.em("g", lambda g, c=c, half=half: g.dma_start(
                    out=o_half[:, c * 1024:(c + 1) * 1024],
                    in_=d["o_wT"][c * 128:(c + 1) * 128,
                                  half * 1024:(half + 1) * 1024]),
                    waits=ww, inc=sgb, k=16)
                wld_list.append(sgb.n)
            wld_all = sgb.n
            for m in range(T):
                ww = [(sgb, wld_all)]
                if half == 0 and m == 0:
                    ww.append((ve, self.bcast_done_ve))
                if m >= 2:
                    ww.append((ac, ev_done[m % 2]))
                for gi in range(2):
                    g0 = (m % 2) * 1024 + gi * 512
                    for c in range(C):
                        P.em("t", lambda t_, c=c, m=m, gi=gi, g0=g0, half=half: t_.matmul(
                            self.ps_s[:, g0:g0 + 512],
                            self.gatedfm[:, c * 2048 + m * 128:c * 2048 + (m + 1) * 128],
                            o_half[:, c * 1024 + gi * 512:c * 1024 + gi * 512 + 512],
                            start=(c == 0), stop=False),
                            waits=ww if (gi == 0 and c == 0) else [], inc=pe)
                    P.em("t", lambda t_, gi=gi, g0=g0, half=half: t_.matmul(
                        self.ps_s[:, g0:g0 + 512], self.ones_rb[0:1, :],
                        obeta_bf[0:1, half * 1024 + gi * 512:
                                 half * 1024 + gi * 512 + 512],
                        start=False, stop=True), inc=pe)
                wmm = pe.n
                ww2 = [(pe, wmm)]
                if m >= 2:
                    ww2.append((ssp[m % 2], sp_done[m % 2]))
                P.em("sc", lambda sc, m=m: sc.copy(
                    bounce[:, (m % 2) * 1024:(m % 2 + 1) * 1024],
                    self.ps_s[:, (m % 2) * 1024:(m % 2 + 1) * 1024]),
                    waits=ww2, inc=ac)
                ev_done[m % 2] = ac.n
                wcp = ac.n
                P.em("s", lambda s, m=m, half=half: s.dma_start(
                    out=self.attn_sp[m * 128:(m + 1) * 128,
                                     half * 1024:(half + 1) * 1024],
                    in_=bounce[:, (m % 2) * 1024:(m % 2 + 1) * 1024]),
                    waits=[(ac, wcp)], inc=ssp[m % 2], k=16)
                sp_done[m % 2] = ssp[m % 2].n
        self.p2_pe, self.p2_ac = pe.n, ac.n
        self.p2_sp = [ssp[0].n, ssp[1].n]

    def ffn_epilogue(self):
        P, d = self.P, self.d
        pe, ac, ve = self.pe, self.ac, self.ve
        sgb, sxp, sout, sconP3 = self.sgb, self.sxp, self.sout, self.sconP3
        self.ctxP2.close()
        self.ctxAB.close()
        self.ctxTP.close()
        self.ctxAB2 = ExitStack()
        ctx = ExitStack()
        self.ctxP3 = ctx
        grep = self.sb(ctx, "grep", [128, D], F32)
        gwf_sb = self.sb(ctx, "gwf_sb", [128, C * 2048], BF16)
        gbf_bf = self.sb(ctx, "gbf_bf", [1, D], BF16)
        gate_sb = self.sb(ctx, "gate_sb", [128, D], F32)
        ubuf = self.sb(ctx, "ubuf", [128, 2 * D], F32)
        xd_st = self.sb(ctx, "xd_st", [128, 2 * D], F32)
        attn_bf = self.sb(ctx, "attn_bf", [128, 2 * D], BF16)
        nmu2 = self.sb(ctx, "nmu2", [128, T], F32)
        va2 = self.sb(ctx, "va2", [128, T], F32)
        rstd2 = self.sb(ctx, "rstd2", [128, T], F32)
        nmr2 = self.sb(ctx, "nmr2", [128, T], F32)
        s2 = self.sb(ctx, "s2", [128, T], F32)
        qt = self.sb(ctx, "qt", [128, 2], F32)
        kmag = self.sb(ctx, "kmag", [128, 1], I32)

        start_w = [(pe, self.p2_pe), (ac, self.p2_ac),
                   (self.ssp[0], self.p2_sp[0]), (self.ssp[1], self.p2_sp[1]),
                   (ve, ve.n)]
        P.em("s", lambda s: s.dma_start(out=ubuf[0:1, 0:D], in_=d["out_g"]),
             waits=start_w, inc=sconP3, k=16)
        wld = sconP3.n
        for gi in range(4):
            P.em("t", lambda t_, gi=gi: t_.matmul(
                self.ps_s[:, gi * 512:(gi + 1) * 512], self.ones_rf[0:1, :],
                ubuf[0:1, gi * 512:(gi + 1) * 512], start=True, stop=True),
                waits=([(sconP3, wld)] + start_w) if gi == 0 else [], inc=pe)
        w1 = pe.n
        P.em("v", lambda v: v.tensor_copy(grep[:], self.ps_s[:, 0:2048]),
             waits=[(pe, w1)], inc=ve)
        wrep = ve.n

        P.em("g", lambda g: g.dma_start(out=gbf_bf[:], in_=d["gbf"]),
             waits=[(pe, w1), (ac, self.p2_ac),
                    (self.ssp[0], self.p2_sp[0]), (self.ssp[1], self.p2_sp[1])],
             inc=sgb, k=16)
        wgwf_c = []
        for c in range(C):
            P.em("g", lambda g, c=c: g.dma_start(
                out=gwf_sb[:, c * 2048:(c + 1) * 2048],
                in_=d["gwf"][c * 128:(c + 1) * 128, :]), inc=sgb, k=16)
            wgwf_c.append(sgb.n)
        wgwf = sgb.n

        P.em("v", lambda v: v.memset(kmag[:], QMAGIC), waits=[(ve, wrep)])

        sig_last = 0
        u_ve = [0] * T
        xa_ve = [0] * T
        xn_ac = [0] * T
        sq_ac = [0] * T
        quake_ve = [0] * T
        fin_ac = [0] * T
        outadd_ve = [0] * T
        outd = [0, 0]
        xld = [0, 0]

        def finish_act(mp):
            sl = mp % 2
            ub = ubuf[:, sl * D:(sl + 1) * D]
            P.em("sc", lambda sc, ub=ub, mp=mp: sc.activation(
                ub, ub, AF.Identity,
                bias=nmr2[:, mp:mp + 1], scale=rstd2[:, mp:mp + 1]),
                waits=[(ve, quake_ve[mp])], inc=ac)
            fin_ac[mp] = ac.n

        def finish_dve(mp):
            sl = mp % 2
            ub = ubuf[:, sl * D:(sl + 1) * D]
            P.em("v", lambda v, ub=ub: v.tensor_tensor(ub, ub, grep[:],
                                                       ALU.mult),
                 waits=[(ac, fin_ac[mp])], inc=ve)
            P.em("v", lambda v, ub=ub, sl=sl: v.tensor_tensor(
                ub, ub, xd_st[:, sl * D:(sl + 1) * D], ALU.add),
                inc=ve, nofence=True)
            outadd_ve[mp] = ve.n
            P.em("s", lambda s, mp=mp, ub=ub: s.dma_start(
                out=self.out[mp * 128:(mp + 1) * 128, :], in_=ub),
                waits=[(ve, outadd_ve[mp])], inc=sout, k=16)
            outd[mp % 2] = sout.n

        for m in range(T):
            sl = m % 2
            wwx = [(ve, wrep)] if m < 2 else [(ve, outadd_ve[m - 2])]
            P.em("s", lambda s, m=m, sl=sl: s.dma_start(
                out=xd_st[:, sl * D:(sl + 1) * D],
                in_=d["x"][m * 128:(m + 1) * 128, :]), waits=wwx,
                inc=sxp[sl], k=16)
            xld[sl] = sxp[sl].n
            P.em("s", lambda s, m=m, sl=sl: s.dma_start(
                out=attn_bf[:, sl * D:(sl + 1) * D],
                in_=self.attn_sp[m * 128:(m + 1) * 128, :]),
                waits=[(ve, xa_ve[m - 2]), (ac, sq_ac[m - 2])] if m >= 2
                else [], inc=sxp[sl], k=16)
            wat = sxp[sl].n
            if m == 0:
                for c in range(C):
                    for gi in range(4):
                        g0 = gi * 512
                        ww = [(sgb, wgwf_c[c])] if gi == 0 else []
                        if c == 0 and gi == 0:
                            ww += [(ac, sig_last), (ve, wrep)]
                        P.em("t", lambda t_, c=c, g0=g0: t_.matmul(
                            self.ps_s[:, g0:g0 + 512],
                            self.xnfm[:, c * 2048:c * 2048 + 128],
                            gwf_sb[:, c * 2048 + g0:c * 2048 + g0 + 512],
                            start=(c == 0), stop=False), waits=ww, inc=pe)
                for gi in range(4):
                    g0 = gi * 512
                    P.em("t", lambda t_, g0=g0: t_.matmul(
                        self.ps_s[:, g0:g0 + 512], self.ones_rb[0:1, :],
                        gbf_bf[0:1, g0:g0 + 512], start=False, stop=True),
                        inc=pe)
            else:
                for gi in range(4):
                    g0 = gi * 512
                    ww = []
                    if gi == 0:
                        ww = [(sgb, wgwf), (ac, sig_last)]
                    for c in range(C):
                        P.em("t", lambda t_, c=c, m=m, g0=g0: t_.matmul(
                            self.ps_s[:, g0:g0 + 512],
                            self.xnfm[:, c * 2048 + m * 128:c * 2048 + (m + 1) * 128],
                            gwf_sb[:, c * 2048 + g0:c * 2048 + g0 + 512],
                            start=(c == 0), stop=False),
                            waits=ww if c == 0 else [], inc=pe)
                        ww = []
                    P.em("t", lambda t_, g0=g0: t_.matmul(
                        self.ps_s[:, g0:g0 + 512], self.ones_rb[0:1, :],
                        gbf_bf[0:1, g0:g0 + 512], start=False, stop=True), inc=pe)
            wmm = pe.n
            wws = [(pe, wmm)]
            if m >= 1:
                wws.append((ve, u_ve[m - 1]))
            P.em("sc", lambda sc: sc.activation(gate_sb[:], self.ps_s[:, 0:2048],
                                                AF.Sigmoid), waits=wws, inc=ac)
            wsg_ = ac.n
            sig_last = ac.n
            wwn = [(sxp[sl], xld[sl])]
            if m >= 2:
                wwn.append((sout, outd[sl]))
            P.em("sc", lambda sc, m=m, sl=sl: sc.activation(
                ubuf[:, sl * D:(sl + 1) * D],
                xd_st[:, sl * D:(sl + 1) * D], AF.Identity,
                bias=self.nmr_st[:, m:m + 1], scale=self.rstd_st[:, m:m + 1]),
                waits=wwn, inc=ac)
            xn_ac[m] = ac.n
            if m >= 1:
                finish_act(m - 1)
            P.em("v", lambda v: v.tensor_tensor(gate_sb[:], gate_sb[:],
                                                self.bcast_rep[:], ALU.mult),
                 waits=[(ac, wsg_)], inc=ve)
            P.em("v", lambda v, sl=sl: v.tensor_tensor(
                ubuf[:, sl * D:(sl + 1) * D],
                ubuf[:, sl * D:(sl + 1) * D], gate_sb[:], ALU.add),
                waits=[(ac, xn_ac[m])], inc=ve)
            u_ve[m] = ve.n
            if m >= 1:
                finish_dve(m - 1)
            P.em("v", lambda v, sl=sl: v.tensor_tensor(
                xd_st[:, sl * D:(sl + 1) * D],
                xd_st[:, sl * D:(sl + 1) * D],
                attn_bf[:, sl * D:(sl + 1) * D], ALU.add),
                waits=[(sxp[sl], wat), (ac, xn_ac[m])], inc=ve)
            xa_ve[m] = ve.n
            P.em("sc", lambda sc, m=m, sl=sl: sc.activation(
                gate_sb[:], ubuf[:, sl * D:(sl + 1) * D], AF.Identity,
                accum_out=s2[:, m:m + 1]), waits=[(ve, u_ve[m])], inc=ac)
            wus = ac.n
            P.em("v", lambda v, m=m: v.tensor_scalar_mul(
                nmu2[:, m:m + 1], s2[:, m:m + 1], -1.0 / D),
                waits=[(ac, wus)])
            wng = ve.n
            P.em("sc", lambda sc, m=m, sl=sl: sc.activation(
                attn_bf[:, sl * D:(sl + 1) * D],
                ubuf[:, sl * D:(sl + 1) * D], AF.Square,
                bias=nmu2[:, m:m + 1], accum_out=va2[:, m:m + 1]),
                waits=[(ve, wng), (ve, xa_ve[m])], inc=ac)
            sq_ac[m] = ac.n
            yc = rstd2[:, m:m + 1]
            yi = rstd2.bitcast(I32)[:, m:m + 1]
            tc = qt[:, 1:2]
            P.em("v", lambda v, m=m: v.tensor_scalar(
                qt[:, 0:1], va2[:, m:m + 1], 1.0 / D, EPS2, ALU.mult, ALU.add),
                waits=[(ac, sq_ac[m])])
            P.em("v", lambda v, yi=yi: v.tensor_scalar(
                yi, qt.bitcast(I32)[:, 0:1], 1, None, ALU.logical_shift_right))
            P.em("v", lambda v, yi=yi: v.tensor_tensor(
                yi, kmag[:], yi, ALU.subtract))
            for _ in range(3):
                P.em("v", lambda v, yc=yc, tc=tc: v.tensor_tensor(
                    tc, qt[:, 0:1], yc, ALU.mult))
                P.em("v", lambda v, yc=yc, tc=tc: v.tensor_tensor(
                    tc, tc, yc, ALU.mult))
                P.em("v", lambda v, tc=tc: v.tensor_scalar(
                    tc, tc, -0.5, 1.5, ALU.mult, ALU.add))
                P.em("v", lambda v, yc=yc, tc=tc: v.tensor_tensor(
                    yc, yc, tc, ALU.mult))
            P.em("v", lambda v, m=m, yc=yc: v.tensor_tensor(
                nmr2[:, m:m + 1], nmu2[:, m:m + 1], yc, ALU.mult))
            quake_ve[m] = ve.n

        finish_act(T - 1)
        finish_dve(T - 1)
        if self.debug:
            P.em("s", lambda s: s.dma_start(out=self.dbg["dbg_attn"],
                                            in_=self.attn_sp),
                 waits=[(self.ssp[0], self.ssp[0].n),
                        (self.ssp[1], self.ssp[1].n)], inc=self.sout, k=16)
        P.em("s", lambda s: s.nop(), waits=[(sout, sout.n)])

    def build(self):
        self.declare_io()
        self.alloc_outer()
        self.load_consts()
        self.thought_alloc()
        self.phase0()
        self.thought_prologue()
        self.pre_attn_pe = self.pe.n
        self.pre_attn_ve = self.ve.n
        self.pre_attn_ac = self.ac.n
        if self.debug:
            P, ve, sout = self.P, self.ve, self.sout
            P.em("s", lambda s: s.dma_start(out=self.dbg["dbg_xnfm"],
                                            in_=self.xnfm[:]),
                 waits=[(ve, ve.n)], inc=sout, k=16)
        self.P.flush(self.nc, "b0")
        self.ctxT.close()
        self.ctxP0.close()
        self.attention()
        self.P.flush(self.nc, "b1")
        self.oproj()
        self.P.flush(self.nc, "b2")
        self.ffn_epilogue()
        self.P.flush(self.nc, "b3")
        self.ctxP3.close()
        self.ctxAB2.close()
        self.ctx0.close()
        return self.nc


def _host_layout(inputs):
    f = np.float32

    def a(x):
        return np.ascontiguousarray(np.asarray(x, dtype=f))

    x_all = a(inputs["L0_attn_x"])
    qw = a(inputs["L0_attn_q_w"])
    kw = a(inputs["L0_attn_k_w"])
    vw = a(inputs["L0_attn_v_w"])
    ow = a(inputs["L0_attn_o_w"])

    def pack(w):
        return np.ascontiguousarray(
            w.T.reshape(C, 128, H, 128).transpose(1, 2, 0, 3))

    common = {
        "wq_p": pack(qw), "wk_p": pack(kw), "wv_p": pack(vw),
        "o_wT": a(ow.T),
        "gw_aT": a(inputs["L0_attn_gate_w"].T),
        "gb_a": a(inputs["L0_attn_gate_b"]).reshape(1, HD),
        "identity": np.eye(128, dtype=f),
        "ones_row": np.ones((1, 128), f),
        "ones_col": np.ones((128, 1), f),
        "nsl": np.triu(np.full((128, 128), -1e30, f), 1),
        "agg_w": a(inputs["L0_ffn_agg_w"]),
        "d1_w": a(inputs["L0_ffn_d1_w"]),
        "d2_w": a(inputs["L0_ffn_d2_w"]),
        "bc_w": a(inputs["L0_ffn_bc_w"]),
        "gwf": a(inputs["L0_ffn_gate_w"]),
        "gbf": a(inputs["L0_ffn_gate_b"]).reshape(1, D),
        "agg_bT": np.ascontiguousarray(
            a(inputs["L0_ffn_agg_b"]).reshape(C, 128).T),
        "d1bT": np.ascontiguousarray(
            a(inputs["L0_ffn_d1_b"]).reshape(64, 128).T),
        "d2bT": np.ascontiguousarray(
            a(inputs["L0_ffn_d2_b"]).reshape(C, 128).T),
        "bcb": a(inputs["L0_ffn_bc_b"]).reshape(1, D),
        "in_g": a(inputs["L0_ffn_in_gamma"]).reshape(1, D),
        "in_b": a(inputs["L0_ffn_in_beta"]).reshape(1, D),
        "pn_gT": np.ascontiguousarray(
            a(inputs["L0_ffn_pn_gamma"]).reshape(C, 128).T),
        "pn_bT": np.ascontiguousarray(
            a(inputs["L0_ffn_pn_beta"]).reshape(C, 128).T),
        "out_g": a(inputs["L0_ffn_out_gamma"]).reshape(1, D),
        "out_b": a(inputs["L0_ffn_out_beta"]).reshape(1, D),
    }
    in_maps = []
    for b in range(8):
        m = dict(common)
        m["x"] = np.ascontiguousarray(x_all[b])
        in_maps.append(m)
    return in_maps


def kernel(**inputs):
    in_maps = _host_layout(inputs)
    nc = Builder().build()
    res = run_bass_kernel_spmd(nc, in_maps, core_ids=list(range(8)))
    return np.stack([np.asarray(r["out"], np.float32) for r in res.results], axis=0)


if __name__ == "__main__":
    nc = Builder().build()
    print("built ok; instructions:", nc.n_instructions())


# version 27
# speedup vs baseline: 1.1865x; 1.1865x over previous
import math
import numpy as np
import concourse.bass as bass
import concourse.mybir as mybir
from concourse.bass_utils import run_bass_kernel_spmd
from contextlib import ExitStack

F32 = mybir.dt.float32
BF16 = mybir.dt.bfloat16
I32 = mybir.dt.int32
QMAGIC = 0x5F3759DF
AF = mybir.ActivationFunctionType
ALU = mybir.AluOpType
AX = mybir.AxisListType

S = 2048
D = 2048
H = 16
HD = 128
C = 16
T = 16
HID = 8192
EPS1 = 1e-5
EPS2 = 1e-6
SCALE = 1.0 / math.sqrt(HD)
KSTEPS = 4

ENGS = ("t", "sc", "v", "g", "s")


class Ctr:
    def __init__(self, sem):
        self.sem = sem
        self.n = 0


class Prog:

    def __init__(self, ve=None):
        self.ops = {e: [] for e in ENGS}
        self.ve = ve

    def em(self, eng, fn, waits=(), inc=None, k=1, nofence=False):
        waits = list(waits)
        if eng == "v" and self.ve is not None:
            if not nofence:
                waits.append((self.ve, self.ve.n))
            if inc is None:
                inc = self.ve
        w = tuple((c.sem, v) for (c, v) in waits if v > 0)
        self.ops[eng].append((w, fn, None if inc is None else inc.sem, k))
        if inc is not None:
            inc.n += k

    def flush(self, nc, name):
        ops = self.ops
        self.ops = {e: [] for e in ENGS}

        def body(engine, lst):
            seen = {}
            for w, fn, inc_sem, k in lst:
                for sem, val in w:
                    key = id(sem)
                    if seen.get(key, -1) < val:
                        engine.wait_ge(sem, val)
                        seen[key] = val
                inst = fn(engine)
                if inc_sem is not None:
                    inst.then_inc(inc_sem, k)

        with nc.Block(name) as block:
            if ops["t"]:
                @block.tensor
                def _(t):
                    body(t, ops["t"])
            if ops["sc"]:
                @block.scalar
                def _(sc):
                    body(sc, ops["sc"])
            if ops["v"]:
                @block.vector
                def _(v):
                    body(v, ops["v"])
            if ops["g"]:
                @block.gpsimd
                def _(g):
                    body(g, ops["g"])
            if ops["s"]:
                @block.sync
                def _(s):
                    body(s, ops["s"])


class Builder:
    def __init__(self, debug=False):
        self.debug = debug
        self.nc = bass.Bass("TRN2", target_bir_lowering=False, debug=False,
                            detect_race_conditions=False)
        self.P = Prog()
        self.ctx0 = ExitStack()

    def sb(self, st, name, shape, dt):
        return st.enter_context(self.nc.sbuf_tensor(name, list(shape), dt))

    def declare_io(self):
        nc = self.nc
        self.d = {}
        for name, shape in [
            ("x", (S, D)), ("wq_p", (128, H, C, 128)), ("wk_p", (128, H, C, 128)),
            ("wv_p", (128, H, C, 128)), ("o_wT", (D, D)), ("gw_aT", (HD, HD)),
            ("gb_a", (1, HD)), ("identity", (128, 128)), ("ones_row", (1, 128)),
            ("ones_col", (128, 1)), ("nsl", (128, 128)), ("agg_w", (D, D)),
            ("d1_w", (D, HID)), ("d2_w", (HID, D)), ("bc_w", (D, D)),
            ("gwf", (D, D)), ("gbf", (1, D)), ("agg_bT", (128, C)),
            ("d1bT", (128, 64)), ("d2bT", (128, C)), ("bcb", (1, D)),
            ("in_g", (1, D)), ("in_b", (1, D)), ("pn_gT", (128, C)),
            ("pn_bT", (128, C)), ("out_g", (1, D)), ("out_b", (1, D)),
        ]:
            self.d[name] = nc.dram_tensor(name, list(shape), F32,
                                          kind="ExternalInput").ap()
        self.out = nc.dram_tensor("out", [S, D], F32, kind="ExternalOutput").ap()
        if self.debug:
            self.dbg = {}
            for nm, shape, dt in [
                ("dbg_xnfm", [128, C * 2048], BF16),
                ("dbg_bc", [128, D], F32),
                ("dbg_gfm", [128, H * 2048], BF16),
                ("dbg_attn", [S, D], BF16),
                ("dbg_u", [S, D], F32),
            ]:
                self.dbg[nm] = nc.dram_tensor(nm, shape, dt,
                                              kind="ExternalOutput").ap()
        self.attn_sp = nc.dram_tensor("attn_sp", [S, D], BF16, kind="Internal").ap()
        self.u_sp = nc.dram_tensor("u_sp", [S, D], F32, kind="Internal").ap()
        self.xa_sp = nc.dram_tensor("xa_sp", [S, D], F32, kind="Internal").ap()
        self.d1_sp = nc.dram_tensor("d1_sp", [32, 128, 4096], BF16,
                                    kind="Internal").ap()
        self.d2_sp = nc.dram_tensor("d2_sp", [32, 128, 4096], BF16,
                                    kind="Internal").ap()

    def alloc_outer(self):
        nc, c0 = self.nc, self.ctx0
        self.pe = Ctr(c0.enter_context(nc.semaphore("pe")))
        self.ac = Ctr(c0.enter_context(nc.semaphore("ac")))
        self.ve = Ctr(c0.enter_context(nc.semaphore("ve")))
        self.po = Ctr(c0.enter_context(nc.semaphore("po")))
        self.P.ve = self.ve
        def dc(name):
            return Ctr(c0.enter_context(nc.semaphore(name)))
        self.sgc = dc("sgc")
        self.sgc2 = dc("sgc2")
        self.scon1 = dc("scon1")
        self.sconT = dc("sconT")
        self.sconP3 = dc("sconP3")
        self.sxp = [dc("sxp0"), dc("sxp1")]
        self.swb1 = [dc("swb10"), dc("swb11")]
        self.swb2 = [dc("swb20"), dc("swb21")]
        self.ssp = [dc("ssp0"), dc("ssp1")]
        self.sus = [dc("sus0"), dc("sus1")]
        self.sout = dc("sout")
        self.sgw = [dc("sgw0"), dc("sgw1")]
        self.sgt = [dc("sgt0"), dc("sgt1"), dc("sgt2")]
        self.sgu = [dc("sgu0"), dc("sgu1"), dc("sgu2")]
        self.sgb = dc("sgb")
        self.ps_s = c0.enter_context(nc.psum_tensor("ps_s", [128, 2048], F32))
        self.ps_av = c0.enter_context(nc.psum_tensor("ps_av", [128, 512], F32))
        self.ps_g = c0.enter_context(nc.psum_tensor("ps_g", [128, 512], F32))
        self.ps_tr = c0.enter_context(nc.psum_tensor("ps_tr", [128, 512], F32))
        self.ps_m = c0.enter_context(nc.psum_tensor("ps_m", [128, 512], F32))
        self.xnfm = self.sb(c0, "xnfm", [128, C * 2048], BF16)
        self.bcast_rep = self.sb(c0, "bcast_rep", [128, D], F32)
        self.rstd_st = self.sb(c0, "rstd_st", [128, T], F32)
        self.nmr_st = self.sb(c0, "nmr_st", [128, T], F32)
        self.ident_b = self.sb(c0, "ident_b", [128, 128], BF16)
        self.ones_rb = self.sb(c0, "ones_rb", [1, 128], BF16)
        self.ones_cb = self.sb(c0, "ones_cb", [128, 1], BF16)
        self.ones_rf = self.sb(c0, "ones_rf", [1, 128], F32)
        self.ones_cf = self.sb(c0, "ones_cf", [128, 1], F32)
        self.nsl_b = self.sb(c0, "nsl_b", [128, 128], BF16)
        self.gba_b = self.sb(c0, "gba_b", [1, HD], BF16)
        self.gwa_b = self.sb(c0, "gwa_b", [HD, HD], BF16)

    def xnfm_ck(self, c):
        return self.xnfm[:, c * 2048:(c + 1) * 2048]

    def load_consts(self):
        P, d = self.P, self.d
        for dst, src in [(self.ident_b, "identity"), (self.ones_rb, "ones_row"),
                         (self.ones_cb, "ones_col"), (self.nsl_b, "nsl"),
                         (self.gba_b, "gb_a"), (self.gwa_b, "gw_aT")]:
            P.em("g", lambda g, dst=dst, src=src: g.dma_start(out=dst[:], in_=d[src]),
                 inc=self.sgc, k=16)
        for dst, src in [(self.ones_rf, "ones_row"), (self.ones_cf, "ones_col")]:
            P.em("s", lambda s, dst=dst, src=src: s.dma_start(out=dst[:], in_=d[src]),
                 inc=self.scon1, k=16)

    def phase0(self):
        P, d = self.P, self.d
        pe, ac, ve = self.pe, self.ac, self.ve
        ctx = ExitStack()
        self.ctxP0 = ctx
        x_st = self.sb(ctx, "x_st", [128, 4 * D], F32)
        xn_tm = self.sb(ctx, "xn_tm", [128, 2 * D], BF16)
        sq_buf = self.sb(ctx, "sq_buf", [128, 2 * D], F32)
        nmu = self.sb(ctx, "nmu", [128, 2], F32)
        vacc = self.sb(ctx, "vacc", [128, 4], F32)
        sd_b = self.sb(ctx, "sd_b", [128, 2], F32)

        x_done = [0] * T
        ev1_done = [0] * T
        ev2_done = [0] * T
        tr1_done = [0] * T
        tr2_done = [0] * T
        sq_ac = [0] * T
        red_ve = [0] * T
        neg_ve = [0] * T
        nmr_ve = [0] * T
        xld = [0] * T

        def load(tt, first=False):
            buf = (tt % 4) * D
            w = []
            if tt >= 4:
                w.append((ac, x_done[tt - 4]))
            if tt >= 2:
                w.append((ve, red_ve[tt - 2]))
            sxc = self.sxp[tt % 2]
            P.em("s", lambda s, tt=tt, buf=buf: s.dma_start(
                out=x_st[:, buf:buf + D],
                in_=d["x"][tt * 128:(tt + 1) * 128, :]), waits=w, inc=sxc, k=16)
            xld[tt] = sxc.n

        def stats_head(tt):
            buf = (tt % 4) * D
            pp = tt % 2
            sxc = self.sxp[tt % 2]
            wnm_prev = [(ac, sq_ac[tt - 2])] if tt >= 2 else []
            P.em("v", lambda v, buf=buf, pp=pp: v.reduce_sum(
                nmu[:, pp:pp + 1], x_st[:, buf:buf + D], axis=AX.X, negate=True),
                waits=[(sxc, xld[tt])] + wnm_prev, inc=ve)
            red_ve[tt] = ve.n
            P.em("v", lambda v, pp=pp: v.tensor_scalar_mul(
                nmu[:, pp:pp + 1], nmu[:, pp:pp + 1], 1.0 / D), inc=ve)
            neg_ve[tt] = ve.n

        def stats_square(tt):
            buf = (tt % 4) * D
            pp = tt % 2
            P.em("sc", lambda sc, buf=buf, pp=pp: sc.activation(
                sq_buf[:, pp * D:(pp + 1) * D], x_st[:, buf:buf + D], AF.Square,
                bias=nmu[:, pp:pp + 1], accum_out=vacc[:, pp:pp + 1]),
                waits=[(ve, neg_ve[tt])], inc=ac)
            sq_ac[tt] = ac.n

        def stats_mid(tt):
            pp = tt % 2
            P.em("v", lambda v, pp=pp: v.tensor_scalar(
                vacc[:, 2 + pp:3 + pp], vacc[:, pp:pp + 1], 1.0 / D, EPS1,
                ALU.mult, ALU.add), waits=[(ac, sq_ac[tt])], inc=ve)
            vw2 = ve.n
            P.em("sc", lambda sc, pp=pp: sc.activation(
                sd_b[:, pp:pp + 1], vacc[:, 2 + pp:3 + pp], AF.Ln),
                waits=[(ve, vw2)], inc=ac)
            P.em("sc", lambda sc, tt=tt, pp=pp: sc.activation(
                self.rstd_st[:, tt:tt + 1], sd_b[:, pp:pp + 1], AF.Exp,
                scale=-0.5), inc=ac)
            aw2 = ac.n
            P.em("v", lambda v, tt=tt, pp=pp: v.tensor_tensor(
                self.nmr_st[:, tt:tt + 1], nmu[:, pp:pp + 1],
                self.rstd_st[:, tt:tt + 1], ALU.mult),
                waits=[(ac, aw2)], inc=ve)
            nmr_ve[tt] = ve.n

        def stats_id(tt):
            buf = (tt % 4) * D
            pp = tt % 2
            wx = [(ve, nmr_ve[tt])]
            if tt >= 2:
                wx.append((pe, tr2_done[tt - 2]))
            P.em("sc", lambda sc, tt=tt, buf=buf, pp=pp: sc.activation(
                xn_tm[:, pp * D:(pp + 1) * D], x_st[:, buf:buf + D], AF.Identity,
                bias=self.nmr_st[:, tt:tt + 1],
                scale=self.rstd_st[:, tt:tt + 1]), waits=wx, inc=ac)
            x_done[tt] = ac.n

        load(0)
        load(1)
        stats_head(0)
        stats_square(0)
        stats_mid(0)
        stats_id(0)
        for t in range(T):
            pp = t % 2
            for half in range(2):
                wtr = []
                if half == 0:
                    wtr = [(ac, x_done[t]), (self.sgc, self.sgc.n)]
                    if t >= 1:
                        wtr.append((ve, ev1_done[t - 1]))
                elif t >= 1:
                    wtr = [(ve, ev2_done[t - 1])]
                for c in range(half * 8, half * 8 + 8):
                    P.em("t", lambda t_, c=c, pp=pp: t_.matmul(
                        self.ps_s[:, c * 128:(c + 1) * 128],
                        xn_tm[:, pp * D + c * 128:pp * D + (c + 1) * 128],
                        self.ident_b[:], start=True, stop=True),
                        waits=wtr if c == half * 8 else [], inc=pe)
                if half == 0:
                    tr1_done[t] = pe.n
                else:
                    tr2_done[t] = pe.n
            if t + 2 < T:
                load(t + 2)
            if t + 1 < T:
                stats_head(t + 1)
                stats_square(t + 1)

            def evac(t, half):
                trd = tr1_done[t] if half == 0 else tr2_done[t]
                P.em("v", lambda v, t=t, half=half: v.tensor_copy(
                    self.xnfm[:].rearrange("p (c s) -> p c s", c=C)
                    [:, half * 8:half * 8 + 8, t * 128:(t + 1) * 128],
                    self.ps_s[:, half * 1024:(half + 1) * 1024].rearrange(
                        "p (c s) -> p c s", c=8)),
                    waits=[(pe, trd)], inc=ve)
                if half == 0:
                    ev1_done[t] = ve.n
                else:
                    ev2_done[t] = ve.n

            evac(t, 0)
            evac(t, 1)
            if t + 1 < T:
                stats_mid(t + 1)
                stats_id(t + 1)

    def thought_alloc(self):
        ctxP = ExitStack()
        self.ctxTP = ctxP
        self.thc = self.sb(ctxP, "thc", [128, 3 * 4096], BF16)
        self.rowstash = self.sb(ctxP, "rowstash", [1, 2 * 256], F32)
        self.png_t = self.sb(ctxP, "png_t", [128, C], F32)
        self.pnb_t = self.sb(ctxP, "pnb_t", [128, C], F32)
        self.aggb_t = self.sb(ctxP, "aggb_t", [128, C], F32)
        self.d2b_t = self.sb(ctxP, "d2b_t", [128, C], F32)
        self.d1b_t = self.sb(ctxP, "d1b_t", [128, 64], F32)
        self.tho = self.sb(ctxP, "tho", [128, C], F32)
        self.tho_bf = self.sb(ctxP, "tho_bf", [128, C], BF16)
        self.t_bf = self.sb(ctxP, "t_bf", [128, C], BF16)
        self.tact = self.sb(ctxP, "tact", [128, 64], BF16)
        self.tmp16 = self.sb(ctxP, "tmp16", [128, 64], F32)
        self.sc1 = self.sb(ctxP, "sc1", [1, 8], F32)
        self.ttf = self.sb(ctxP, "ttf", [128, 64], F32)
        self.nm_b = self.sb(ctxP, "nm_b", [128, 1], F32)
        self.rs_b = self.sb(ctxP, "rs_b", [128, 1], F32)

    def thought_prologue(self):
        P, d = self.P, self.d
        pe, ac, ve = self.pe, self.ac, self.ve
        sconT, sgc2 = self.sconT, self.sgc2
        sgu = self.sgu
        png_t, pnb_t, aggb_t = self.png_t, self.pnb_t, self.aggb_t
        d2b_t, d1b_t = self.d2b_t, self.d1b_t
        thc = self.thc
        ctx = ExitStack()
        self.ctxT = ctx
        xl = self.sb(ctx, "xl", [1, D], F32)
        xl2 = self.sb(ctx, "xl2", [1, D], F32)
        xl_bf = self.sb(ctx, "xl_bf", [1, D], BF16)
        ing_r = self.sb(ctx, "ing_r", [1, D], F32)
        inb_r = self.sb(ctx, "inb_r", [1, D], F32)
        xnlT = self.sb(ctx, "xnlT", [128, C], BF16)

        P.em("s", lambda s: s.dma_start(out=xl[:], in_=d["x"][S - 1:S, :]),
             inc=sconT, k=16)
        for dst, src in [(ing_r, "in_g"), (inb_r, "in_b"), (png_t, "pn_gT"),
                         (pnb_t, "pn_bT"), (aggb_t, "agg_bT"), (d2b_t, "d2bT"),
                         (d1b_t, "d1bT")]:
            P.em("s", lambda s, dst=dst, src=src: s.dma_start(out=dst[:], in_=d[src]),
                 inc=sconT, k=16)
        trow_sx = sconT.n
        self.trow_sx = trow_sx

        ps_m = self.ps_m
        sc1 = self.sc1

        def row_ln(src, dst, eps):
            P.em("v", lambda v: v.reduce_sum(sc1[:, 0:1], src[:], axis=AX.X,
                                             negate=True), waits=[(ac, ac.n)])
            P.em("v", lambda v: v.tensor_scalar_mul(sc1[:, 0:1], sc1[:, 0:1], 1.0 / D))
            w1 = ve.n
            P.em("sc", lambda sc: sc.activation(dst[:], src[:], AF.Square,
                                                bias=sc1[:, 0:1],
                                                accum_out=sc1[:, 1:2]),
                 waits=[(ve, w1)], inc=ac)
            w2 = ac.n
            P.em("v", lambda v: v.tensor_scalar(sc1[:, 1:2], sc1[:, 1:2],
                                                1.0 / D, eps, ALU.mult, ALU.add),
                 waits=[(ac, w2)])
            w3 = ve.n
            P.em("sc", lambda sc: sc.activation(sc1[:, 2:3], sc1[:, 1:2], AF.Ln),
                 waits=[(ve, w3)], inc=ac)
            P.em("sc", lambda sc: sc.activation(sc1[:, 3:4], sc1[:, 2:3], AF.Exp,
                                                scale=-0.5), inc=ac)
            w4 = ac.n
            P.em("v", lambda v: v.tensor_tensor(sc1[:, 4:5], sc1[:, 0:1],
                                                sc1[:, 3:4], ALU.mult),
                 waits=[(ac, w4)])
            w5 = ve.n
            P.em("sc", lambda sc: sc.activation(dst[:], src[:], AF.Identity,
                                                bias=sc1[:, 4:5], scale=sc1[:, 3:4]),
                 waits=[(ve, w5)], inc=ac)

        P.em("v", lambda v: v.tensor_copy(xl2[:], xl[:]), waits=[(sconT, trow_sx)])
        row_ln(xl, xl2, EPS1)
        row_ln(xl2, xl, EPS2)
        P.em("v", lambda v: v.tensor_tensor(xl[:], xl[:], ing_r[:], ALU.mult),
             waits=[(ac, ac.n)])
        P.em("v", lambda v: v.tensor_tensor(xl_bf[:], xl[:], inb_r[:], ALU.add))
        wxl = ve.n
        for c in range(C):
            P.em("t", lambda t_, c=c: t_.matmul(
                ps_m[:, 96 + c:97 + c], xl_bf[0:1, c * 128:(c + 1) * 128],
                self.ones_rb[0:1, 0:1], start=True, stop=True),
                waits=[(ve, wxl), (self.sgc, self.sgc.n),
                       (self.scon1, self.scon1.n)] if c == 0 else [], inc=pe)
        wp = pe.n
        P.em("v", lambda v: v.tensor_copy(xnlT[:], ps_m[:, 96:112]),
             waits=[(pe, wp)])
        wxnl = ve.n

        self.th_slot = 0
        self.th_slot_pe = [0, 0, 0]
        self.th_slot_wb = [0, 0, 0]
        self.th_last_ld_sem = [None, None, None]
        self.th_ev = [0]
        self.th_cp = [0]
        self.th_g = 0
        self.pend_close = None
        self.wb_sems = [self.swb1[0], self.swb1[1], self.swb2[0]]

        def unit_load(src_ap, n_el, wb_dst=None, sync_q=False):
            sl = self.th_slot % 3
            self.th_slot += 1
            sgp = (self.sgu if sync_q else self.sgt)[sl]
            ww = []
            if self.th_slot_pe[sl]:
                ww.append((pe, self.th_slot_pe[sl]))
            if self.th_slot_wb[sl]:
                ww.append((self.wb_sems[sl], self.th_slot_wb[sl]))
            eng = "s" if sync_q else "g"
            P.em(eng, lambda e, src_ap=src_ap, sl=sl, n_el=n_el: e.dma_start(
                out=self.thc[:, sl * 4096:sl * 4096 + n_el].rearrange(
                    "p (c n) -> p c n", c=src_ap.shape[1]) if len(src_ap.shape) == 3
                else self.thc[:, sl * 4096:sl * 4096 + n_el], in_=src_ap),
                waits=ww, inc=sgp, k=16)
            wld = sgp.n
            if wb_dst is not None:
                P.em("s", lambda s, wb_dst=wb_dst, sl=sl, n_el=n_el: s.dma_start(
                    out=wb_dst, in_=self.thc[:, sl * 4096:sl * 4096 + n_el]),
                    waits=[(sgp, wld)], inc=self.wb_sems[sl], k=16)
                self.th_slot_wb[sl] = self.wb_sems[sl].n
            self.th_last_ld_sem[sl] = sgp
            return sl, wld

        def row_mms(sl, wld, lhs_col_fn, blk0, nblk, start, stop, first_w=()):
            if start:
                flush_close()
            sgp = self.th_last_ld_sem[sl]
            for b in range(nblk):
                ww = []
                if b == 0:
                    ww = [(sgp, wld)] + list(first_w)
                    if start:
                        ww.append((ac, self.th_ev[0]))
                        ww.append((ve, self.th_cp[0]))
                P.em("t", lambda t_, sl=sl, b=b, blk0=blk0, start=start,
                     stop=stop, nblk=nblk: t_.matmul(
                    ps_m[0:1, 0:512],
                    lhs_col_fn(blk0 + b),
                    self.thc[:, sl * 4096 + b * 512:sl * 4096 + (b + 1) * 512],
                    start=(start and b == 0), stop=(stop and b == nblk - 1)),
                    waits=ww, inc=pe)

        def row_close(dst_cols):
            wg = pe.n
            P.em("sc", lambda sc: sc.copy(self.rowstash[0:1, 0:512],
                                          ps_m[0:1, 0:512]),
                 waits=[(pe, wg)], inc=ac)
            self.th_ev[0] = ac.n
            self.pend_close = (dst_cols, ac.n)
            flush_close()

        def flush_close():
            if self.pend_close is None:
                return
            dst_cols, wev = self.pend_close
            self.pend_close = None
            for hh in range(4):
                P.em("t", lambda t_, hh=hh: t_.matmul(
                    ps_m[:, hh:hh + 1],
                    self.rowstash[0:1, hh * 128:(hh + 1) * 128],
                    self.ones_cf[0:1, 0:1], start=True, stop=True),
                    waits=[(ac, wev)] if hh == 0 else [], inc=pe)
            wtr = pe.n
            P.em("v", lambda v, dst_cols=dst_cols: v.tensor_copy(
                self.ttf[:, dst_cols:dst_cols + 4], ps_m[:, 0:4]),
                waits=[(pe, wtr)], inc=ve, nofence=True)
            self.th_cp[0] = ve.n

        self.unit_load = unit_load
        self.row_mms = row_mms
        self.row_close = row_close
        self.flush_close = flush_close

        for ng in range(4):
            srcA = d["agg_w"][0:1024, ng * 512:(ng + 1) * 512].rearrange(
                "(c p) n -> p c n", p=128)
            srcB = d["agg_w"][1024:2048, ng * 512:(ng + 1) * 512].rearrange(
                "(c p) n -> p c n", p=128)
            slA, wldA = unit_load(srcA, 4096)
            slB, wldB = unit_load(srcB, 4096)
            fw = [(ve, wxnl)] if ng == 0 else ()
            row_mms(slA, wldA, lambda b: xnlT[:, b:b + 1], 0, 8, True, False,
                    first_w=fw)
            row_mms(slB, wldB, lambda b: xnlT[:, b:b + 1], 8, 8, False, True)
            row_close(ng * 4)
            self.th_slot_pe[slA] = pe.n
            self.th_slot_pe[slB] = pe.n
        flush_close()
        P.em("v", lambda v: v.tensor_tensor(self.tho[:], self.ttf[:, 0:16],
                                            aggb_t[:], ALU.add),
             waits=[(sconT, trow_sx)])
        self._th_gen = self._thought_gen()

    def _thought_gen(self):
        P, d = self.P, self.d
        pe, ac, ve = self.pe, self.ac, self.ve
        ps_m = self.ps_m
        sc1, nm_b, rs_b = self.sc1, self.nm_b, self.rs_b
        tho, tho_bf, t_bf, tact, tmp16 = (self.tho, self.tho_bf, self.t_bf,
                                          self.tact, self.tmp16)

        for step in range(KSTEPS):
            P.em("v", lambda v: v.reduce_sum(nm_b[:], tho[:], axis=AX.X,
                                             negate=True))
            w1 = ve.n
            P.em("t", lambda t_: t_.matmul(ps_m[0:1, 112:113], self.ones_cf[:],
                                           nm_b[:], start=True, stop=True),
                 waits=[(ve, w1)], inc=pe)
            w2 = pe.n
            P.em("v", lambda v: v.tensor_scalar_mul(sc1[:, 0:1], ps_m[0:1, 112:113],
                                                    1.0 / D), waits=[(pe, w2)])
            w3 = ve.n
            P.em("t", lambda t_: t_.matmul(ps_m[:, 113:114], self.ones_rf[0:1, :],
                                           sc1[0:1, 0:1], start=True, stop=True),
                 waits=[(ve, w3)], inc=pe)
            w4 = pe.n
            P.em("v", lambda v: v.tensor_copy(nm_b[:], ps_m[:, 113:114]),
                 waits=[(pe, w4)])
            w5 = ve.n
            P.em("sc", lambda sc: sc.activation(tmp16[:, 0:C], tho[:], AF.Square,
                                                bias=nm_b[:], accum_out=rs_b[:]),
                 waits=[(ve, w5)], inc=ac)
            w6 = ac.n
            P.em("t", lambda t_: t_.matmul(ps_m[0:1, 112:113], self.ones_cf[:],
                                           rs_b[:], start=True, stop=True),
                 waits=[(ac, w6)], inc=pe)
            w7 = pe.n
            P.em("v", lambda v: v.tensor_scalar(sc1[:, 1:2], ps_m[0:1, 112:113],
                                                1.0 / D, EPS2, ALU.mult, ALU.add),
                 waits=[(pe, w7)])
            w8 = ve.n
            P.em("sc", lambda sc: sc.activation(sc1[:, 2:3], sc1[:, 1:2], AF.Ln),
                 waits=[(ve, w8)], inc=ac)
            P.em("sc", lambda sc: sc.activation(sc1[:, 3:4], sc1[:, 2:3], AF.Exp,
                                                scale=-0.5), inc=ac)
            w9 = ac.n
            P.em("v", lambda v: v.nop(), waits=[(ac, w9)])
            w10 = ve.n
            P.em("t", lambda t_: t_.matmul(ps_m[:, 113:114], self.ones_rf[0:1, :],
                                           sc1[0:1, 3:4], start=True, stop=True),
                 waits=[(ve, w10)], inc=pe)
            w11 = pe.n
            P.em("v", lambda v: v.tensor_copy(rs_b[:], ps_m[:, 113:114]),
                 waits=[(pe, w11)])
            P.em("v", lambda v: v.tensor_scalar(tmp16[:, 0:C], tho[:], nm_b[:],
                                                rs_b[:], ALU.add, ALU.mult))
            P.em("v", lambda v: v.tensor_tensor(tmp16[:, 0:C], tmp16[:, 0:C],
                                                self.png_t[:], ALU.mult))
            P.em("v", lambda v: v.tensor_tensor(t_bf[:], tmp16[:, 0:C],
                                                self.pnb_t[:], ALU.add))
            wtn = ve.n
            yield

            if step == 0:
                self.d1_wb_marks = [None] * 32
            for ng in range(16):
                pair = []
                for hf in range(2):
                    idx = ng * 2 + hf
                    if step == 0:
                        sap = d["d1_w"][hf * 1024:(hf + 1) * 1024,
                                        ng * 512:(ng + 1) * 512].rearrange(
                            "(c p) n -> p c n", p=128)
                        sl, wld = self.unit_load(sap, 4096,
                                                 wb_dst=self.d1_sp[idx])
                        self.d1_wb_marks[idx] = (self.wb_sems[sl],
                                                 self.wb_sems[sl].n)
                    else:
                        sl, wld = self.unit_load(self.d1_sp[idx], 4096,
                                                 sync_q=True)
                        if step == 1:
                            wbs, wbn = self.d1_wb_marks[idx]
                            P.ops["s"][-1] = (P.ops["s"][-1][0]
                                              + ((wbs.sem, wbn),),) + \
                                P.ops["s"][-1][1:]
                    pair.append((sl, wld))
                    yield
                (slA, wldA), (slB, wldB) = pair
                fw = [(ve, wtn)] if ng == 0 else ()
                self.row_mms(slA, wldA, lambda b: t_bf[:, b:b + 1], 0, 8,
                             True, False, first_w=fw)
                self.row_mms(slB, wldB, lambda b: t_bf[:, b:b + 1], 8, 8,
                             False, True)
                self.row_close(ng * 4)
                self.th_slot_pe[slA] = pe.n
                self.th_slot_pe[slB] = pe.n
            self.flush_close()
            P.em("v", lambda v: v.tensor_tensor(self.ttf[:], self.ttf[:],
                                                self.d1b_t[:], ALU.add),
                 waits=[(self.sconT, self.trow_sx)])
            wb1 = ve.n
            P.em("sc", lambda sc: sc.activation(tact[:], self.ttf[:], AF.Gelu),
                 waits=[(ve, wb1)], inc=ac)
            wge = ac.n
            yield

            upd_wait = ve.n
            if step == 0:
                self.d2_wb_marks = [None] * 32
            for ng2 in range(4):
                for q in range(8):
                    idx = ng2 * 8 + q
                    if step == 0:
                        sap = d["d2_w"][q * 1024:(q + 1) * 1024,
                                        ng2 * 512:(ng2 + 1) * 512].rearrange(
                            "(c p) n -> p c n", p=128)
                        sl, wld = self.unit_load(sap, 4096,
                                                 wb_dst=self.d2_sp[idx])
                        self.d2_wb_marks[idx] = (self.wb_sems[sl],
                                                 self.wb_sems[sl].n)
                    else:
                        sl, wld = self.unit_load(self.d2_sp[idx], 4096,
                                                 sync_q=True)
                        if step == 1:
                            wbs, wbn = self.d2_wb_marks[idx]
                            P.ops["s"][-1] = (P.ops["s"][-1][0]
                                              + ((wbs.sem, wbn),),) + \
                                P.ops["s"][-1][1:]
                    fw = []
                    if ng2 == 0 and q == 0:
                        fw = [(ac, wge), (ve, upd_wait)]
                    self.row_mms(sl, wld, lambda b: tact[:, b:b + 1], q * 8, 8,
                                 q == 0, q == 7, first_w=fw)
                    if q == 7:
                        self.row_close(16 + ng2 * 4)
                    self.th_slot_pe[sl] = pe.n
                    yield
            self.flush_close()
            P.em("v", lambda v: v.tensor_tensor(tmp16[:, 0:C],
                                                self.ttf[:, 16:32],
                                                self.d2b_t[:], ALU.add))
            P.em("v", lambda v: v.tensor_tensor(tho[:], tho[:], tmp16[:, 0:C],
                                                ALU.add))
            yield
        P.em("v", lambda v: v.tensor_copy(tho_bf[:], tho[:]))
        self.wtb = ve.n

    def thought_bc(self, ps_free_waits, bcb_bf, bcrow):
        P, d = self.P, self.d
        pe, ac, ve = self.pe, self.ac, self.ve
        ps_m = self.ps_m
        P.em("g", lambda g, bcb_bf=bcb_bf: g.dma_start(out=bcb_bf, in_=d["bcb"]),
             inc=self.sgc2, k=16)
        wbcb = self.sgc2.n
        first = [(ve, self.wtb), (self.sgc2, wbcb)] + list(ps_free_waits)
        for ng in range(4):
            srcA = d["bc_w"][0:1024, ng * 512:(ng + 1) * 512].rearrange(
                "(c p) n -> p c n", p=128)
            srcB = d["bc_w"][1024:2048, ng * 512:(ng + 1) * 512].rearrange(
                "(c p) n -> p c n", p=128)
            slA, wldA = self.unit_load(srcA, 4096)
            slB, wldB = self.unit_load(srcB, 4096)
            fw = first if ng == 0 else ()
            self.row_mms(slA, wldA, lambda b: self.tho_bf[:, b:b + 1],
                         0, 8, True, False, first_w=fw)
            self.row_mms(slB, wldB, lambda b: self.tho_bf[:, b:b + 1],
                         8, 8, False, False)
            off = ng * 512
            P.em("t", lambda t_, off=off, bcb_bf=bcb_bf: t_.matmul(
                ps_m[0:1, 0:512], self.ones_rb[0:1, 0:1],
                bcb_bf[0:1, off:off + 512], start=False, stop=True), inc=pe)
            wg = pe.n
            P.em("sc", lambda sc, off=off, bcrow=bcrow: sc.copy(
                bcrow[0:1, off:off + 512], ps_m[0:1, 0:512]),
                waits=[(pe, wg)], inc=ac)
            self.th_ev[0] = ac.n
            self.th_slot_pe[slA] = pe.n
            self.th_slot_pe[slB] = pe.n
        wrow = ac.n
        for gi in range(4):
            P.em("t", lambda t_, gi=gi, bcrow=bcrow: t_.matmul(
                self.ps_s[:, gi * 512:(gi + 1) * 512], self.ones_rf[0:1, :],
                bcrow[0:1, gi * 512:(gi + 1) * 512], start=True, stop=True),
                waits=[(ac, wrow)] if gi == 0 else [], inc=pe)
        wrep = pe.n
        P.em("v", lambda v: v.tensor_copy(self.bcast_rep[:], self.ps_s[:, 0:2048]),
             waits=[(pe, wrep)])
        self.bcast_done_ve = self.ve.n

    def attention(self):
        P, d = self.P, self.d
        pe, ac, ve = self.pe, self.ac, self.ve
        sgw = self.sgw
        ctxAB = ExitStack()
        self.ctxAB = ctxAB
        self.gatedfm = self.sb(ctxAB, "gatedfm", [128, H * 2048], BF16)
        ctx = ExitStack()
        self.ctxP1 = ctx
        wq_s = self.sb(ctx, "wq_s", [128, 2 * 2048], BF16)
        wk_s = self.sb(ctx, "wk_s", [128, 2 * 2048], BF16)
        wv_s = self.sb(ctx, "wv_s", [128, 2 * 2048], BF16)
        q_fm = self.sb(ctx, "q_fm", [128, 2048], BF16)
        k_fm = self.sb(ctx, "k_fm", [128, 2048], BF16)

        v_aug = self.sb(ctx, "v_aug", [128, T * 132], BF16)
        p_t = self.sb(ctx, "p_t", [128, 2048], BF16)
        v_fm = p_t
        th_sb = self.sb(ctx, "th_sb", [128, 256], F32)
        th1_sb = self.sb(ctx, "th1_sb", [128, 128], F32)
        gated_bf = self.sb(ctx, "gated_bf", [128, 128], BF16)

        nmx = self.sb(ctx, "nmx", [1, 4], F32)
        negm_b = self.sb(ctx, "negm_b", [128, 1], F32)
        rs2 = self.sb(ctx, "rs2", [128, 1], F32)
        recip = self.sb(ctx, "recip", [128, 1], F32)

        start_w = [(pe, self.pre_attn_pe), (ve, self.pre_attn_ve),
                   (ac, self.pre_attn_ac),
                   (self.swb1[0], self.swb1[0].n), (self.swb1[1], self.swb1[1].n),
                   (self.swb2[0], self.swb2[0].n), (self.swb2[1], self.swb2[1].n)]
        P.em("v", lambda v: v.memset(
            v_aug[:].rearrange("p (j w) -> p j w", w=132)[:, :, 128:129], 1.0),
            waits=start_w, inc=ve)
        vones = ve.n

        wslot_pe = [0, 0]
        exp_done = 0
        stt_done = [0] * T
        tr_done = [0] * T
        trevac_done = [0] * T

        for h in range(H):
            buf = (h % 2) * 2048
            sgp = sgw[h % 2]
            ww = [(pe, wslot_pe[h % 2])] if h >= 2 else list(start_w)
            for wdst, wsrc in [(wq_s, "wq_p"), (wk_s, "wk_p"), (wv_s, "wv_p")]:
                P.em("g", lambda g, wdst=wdst, wsrc=wsrc, h=h, buf=buf: g.dma_start(
                    out=wdst[:, buf:buf + 2048],
                    in_=d[wsrc][:, h].rearrange("p c e -> p (c e)")),
                    waits=ww, inc=sgp, k=16)
                ww = []
            wld = sgp.n

            def proj(wsl, dst, hw0, hw1):
                halves = []
                for gh in range(2):
                    for gi in range(2):
                        g0 = gh * 1024 + gi * 512
                        for c in range(C):
                            ww2 = []
                            if gi == 0 and c == 0:
                                ww2 = list(hw0 if gh == 0 else hw1)
                            P.em("t", lambda t_, c=c, g0=g0, wsl=wsl, buf=buf: t_.matmul(
                                self.ps_s[:, g0:g0 + 512],
                                wsl[:, buf + c * 128:buf + (c + 1) * 128],
                                self.xnfm[:, c * 2048 + g0:c * 2048 + g0 + 512],
                                start=(c == 0), stop=(c == C - 1)),
                                waits=ww2, inc=pe)
                    wg = pe.n
                    P.em("v", lambda v, gh=gh, dst=dst: v.tensor_copy(
                        dst[:, gh * 1024:(gh + 1) * 1024],
                        self.ps_s[:, gh * 1024:(gh + 1) * 1024]),
                        waits=[(pe, wg)], inc=ve)
                    halves.append(ve.n)
                return halves

            if h > 0:
                wq0 = [(sgp, wld), (ac, self.exp_h1_prev)]
                wq1 = [(ac, exp_done)]
            else:
                wq0 = [(sgp, wld), (ve, self.pre_attn_ve)]
                wq1 = []
            qh = proj(wq_s, q_fm, wq0, wq1)
            kh = proj(wk_s, k_fm, [(ve, qh[0])], [(ve, qh[1])])
            vh = proj(wv_s, v_fm, [(ve, kh[0])], [(ve, kh[1])])

            for j in range(T):
                ww3 = [(ve, vh[j // 8])]
                if j < 2:
                    if j == 0:
                        ww3.append((ve, vones))
                    if h > 0:
                        ww3.append((ve, self.trevac_prev))
                        ww3.append((ve, self.stt_prev_head))
                if j >= 2:
                    ww3.append((ve, vtr_ev[j - 2]))
                vps = self.ps_tr if (j % 2 == 0) else self.ps_av
                P.em("t", lambda t_, j=j, vps=vps: t_.matmul(
                    vps[:, 0:128],
                    v_fm[:, j * 128:(j + 1) * 128], self.ident_b[:],
                    start=True, stop=True), waits=ww3, inc=pe)
                wv1 = pe.n
                P.em("v", lambda v, j=j, vps=vps: v.tensor_copy(
                    v_aug[:, j * 132:j * 132 + 128], vps[:, 0:128]),
                    waits=[(pe, wv1)], inc=ve)
                if j == 0:
                    vtr_ev = [0] * T
                vtr_ev[j] = ve.n

            for idx, (src_fm, done) in enumerate([(q_fm, qh[1]), (k_fm, kh[1])]):
                P.em("v", lambda v, src_fm=src_fm: v.tensor_tensor(
                    p_t[:], src_fm[:], src_fm[:], ALU.mult),
                    waits=[(ve, done), (pe, nrm_pe[idx] if False else 0)], inc=ve)
                wq2 = ve.n
                for gi in range(4):
                    ww = []
                    if gi == 0:
                        ww = [(ve, wq2), (ve, vh[1])]
                        if idx == 1:
                            ww.append((ve, red_q))
                    P.em("t", lambda t_, gi=gi: t_.matmul(
                        self.ps_s[0:1, gi * 512:(gi + 1) * 512], self.ones_cb[:],
                        p_t[:, gi * 512:(gi + 1) * 512], start=True, stop=True),
                        waits=ww, inc=pe)
                wmm_ = pe.n
                P.em("v", lambda v, idx=idx: v.reduce_max(
                    nmx[:, idx:idx + 1], self.ps_s[0:1, 0:2048], axis=AX.X),
                    waits=[(pe, wmm_)], inc=ve)
                red_q = ve.n
            P.em("v", lambda v: v.tensor_tensor(nmx[:, 2:3], nmx[:, 0:1],
                                                nmx[:, 1:2], ALU.add), inc=ve)
            P.em("v", lambda v: v.tensor_scalar_mul(nmx[:, 2:3], nmx[:, 2:3],
                                                    -0.5 * SCALE), inc=ve)
            wnm2 = ve.n
            P.em("t", lambda t_: t_.matmul(self.ps_g[:, 2:3], self.ones_rf[0:1, :],
                                           nmx[0:1, 2:3], start=True, stop=True),
                 waits=[(ve, wnm2), (ac, tanh_prev if h > 0 else 0)], inc=pe)
            wnm3 = pe.n
            P.em("v", lambda v: v.tensor_copy(negm_b[:], self.ps_g[:, 2:3]),
                 waits=[(pe, wnm3)], inc=ve)
            wnegm = ve.n

            st_done = [0] * T
            stc_done = [0, 0]
            expc_done = [exp_done, exp_done]
            gate_done = [0] * T
            av_done = [0] * T
            exp_list = [0] * T
            exp_h1 = [0] * T
            tanh_list = [0] * T
            th1_list = [0] * T

            for m in range(T + 2):
                if 1 <= m <= T:
                    mm = m - 1
                    for j in range(mm + 1):
                        ww4 = []
                        if j == 0:
                            ww4 = [(ac, exp_h1[mm])]
                            if mm == 0:
                                ww4.append((ve, vtr_ev[T - 1]))
                            if mm >= 1:
                                ww4.append((ve, stt_done[mm - 1]))
                            elif h > 0:
                                ww4.append((ve, self.stt_prev_head))
                        elif j == 8:
                            ww4 = [(ac, exp_list[mm])]
                        P.em("t", lambda t_, j=j, mm=mm: t_.matmul(
                            self.ps_av[:, 0:129],
                            p_t[:, j * 128:(j + 1) * 128],
                            v_aug[:, j * 132:j * 132 + 129],
                            start=(j == 0), stop=(j == mm)),
                            waits=ww4, inc=pe)
                    av_done[mm] = pe.n
                if m < T:
                    for half_ in range(2):
                        j0, j1 = (0, min(8, m + 1)) if half_ == 0 else (8, m + 1)
                        if j0 >= j1:
                            continue
                        for j in range(j0, j1):
                            ww5 = []
                            if j == j0:
                                ww5 = [(ac, expc_done[half_]), (ve, wnegm)]
                                if m == 0:
                                    ww5.append((ve, vtr_ev[T - 1]))
                            diag = (j == m)
                            P.em("t", lambda t_, j=j, m=m, diag=diag: t_.matmul(
                                self.ps_s[:, j * 128:(j + 1) * 128],
                                k_fm[:, j * 128:(j + 1) * 128],
                                q_fm[:, m * 128:(m + 1) * 128],
                                start=True, stop=not diag), waits=ww5, inc=pe)
                            if diag:
                                P.em("t", lambda t_, j=j: t_.matmul(
                                    self.ps_s[:, j * 128:(j + 1) * 128],
                                    self.nsl_b[:], self.ident_b[:],
                                    start=False, stop=True), inc=pe)
                        stc_done[half_] = pe.n
                        P.em("sc", lambda sc, m=m, j0=j0, j1=j1: sc.activation(
                            p_t[:, j0 * 128:j1 * 128],
                            self.ps_s[:, j0 * 128:j1 * 128],
                            AF.Exp, bias=negm_b[:], scale=SCALE),
                            waits=[(pe, stc_done[half_])], inc=ac)
                        expc_done[half_] = ac.n
                        if half_ == 0:
                            exp_h1[m] = ac.n
                    st_done[m] = pe.n
                    exp_list[m] = ac.n
                    exp_done = ac.n
                    ww6 = [(ac, tanh_list[m - 1] if m > 0 else
                            (tanh_prev if h > 0 else 0))]
                    if m == 0:
                        ww6.append((ve, wnegm))
                    P.em("t", lambda t_, m=m: t_.matmul(
                        self.ps_g[:, 0:128], q_fm[:, m * 128:(m + 1) * 128],
                        self.gwa_b[:], start=True, stop=False), waits=ww6, inc=pe)
                    P.em("t", lambda t_: t_.matmul(
                        self.ps_g[:, 0:128], self.ones_rb[0:1, :], self.gba_b[0:1, :],
                        start=False, stop=True), inc=pe)
                    gate_done[m] = pe.n
                if 2 <= m:
                    mm2 = m - 2
                    ww7 = [(ve, stt_done[mm2])]
                    if mm2 >= 1:
                        ww7.append((ve, trevac_done[mm2 - 1]))
                    else:
                        ww7.append((ve, vtr_ev[T - 2]))
                        if h > 0:
                            ww7.append((ve, self.trevac_prev))
                    P.em("t", lambda t_, mm2=mm2: t_.matmul(
                        self.ps_tr[:, 0:128],
                        gated_bf[:], self.ident_b[:], start=True, stop=True),
                        waits=ww7, inc=pe)
                    tr_done[mm2] = pe.n
                if m < T:
                    ww8 = [(pe, gate_done[m])]
                    if m >= 2:
                        ww8.append((ve, th1_list[m - 2]))
                    P.em("sc", lambda sc, m=m: sc.activation(
                        th_sb[:, (m % 2) * 128:(m % 2) * 128 + 128],
                        self.ps_g[:, 0:128], AF.Tanh, scale=0.5), waits=ww8, inc=ac)
                    tanh_list[m] = ac.n

                if 1 <= m <= T:
                    mm = m - 1
                    P.em("v", lambda v: v.reciprocal(recip[:], self.ps_av[:, 128:129]),
                         waits=[(pe, av_done[mm])], inc=ve)
                    P.em("v", lambda v, mm=mm: v.tensor_scalar(
                        th1_sb[:], th_sb[:, (mm % 2) * 128:(mm % 2) * 128 + 128],
                        0.5, 0.5, ALU.mult, ALU.add),
                        waits=[(ac, tanh_list[mm])], inc=ve)
                    th1_list[mm] = ve.n
                    ww9 = [(pe, tr_done[mm - 1] if mm >= 1 else 0)]
                    if mm == 0 and h > 0:
                        ww9.append((pe, self.tr_prev_head))
                    P.em("v", lambda v: v.scalar_tensor_tensor(
                        gated_bf[:], self.ps_av[:, 0:128], recip[:], th1_sb[:],
                        ALU.mult, ALU.mult), waits=ww9, inc=ve)
                    stt_done[mm] = ve.n
                if 2 <= m:
                    mm2 = m - 2
                    P.em("v", lambda v, mm2=mm2, h=h: v.tensor_copy(
                        self.gatedfm[:, h * 2048 + mm2 * 128:h * 2048 + (mm2 + 1) * 128],
                        self.ps_tr[:, 0:128]),
                        waits=[(pe, tr_done[mm2])], inc=ve, nofence=True)
                    trevac_done[mm2] = ve.n
                next(self._th_gen, None)

            tanh_prev = tanh_list[T - 1]
            self.exp_h1_prev = exp_h1[T - 1]
            self.stt_prev_head = stt_done[T - 1]
            self.tr_prev_head = tr_done[T - 1]
            self.trevac_prev = trevac_done[T - 1]
            wslot_pe[h % 2] = pe.n
        self.p1_pe, self.p1_ve, self.p1_ac = pe.n, ve.n, ac.n
        self.exp_done_final = exp_done

    def oproj(self):
        P, d = self.P, self.d
        pe, ac, ve = self.pe, self.ac, self.ve
        sgb, ssp = self.sgb, self.ssp
        self.ctxP1.close()
        ctx = ExitStack()
        self.ctxP2 = ctx
        o_half = self.sb(ctx, "o_half", [128, C * 1024], BF16)
        bounce = self.sb(ctx, "bounce", [128, 2 * 1024], BF16)
        obeta_bf = self.sb(ctx, "obeta_bf", [1, D], BF16)

        for _ in self._th_gen:
            pass
        self.thought_bc([(ac, self.p1_ac), (ve, self.p1_ve)],
                        bounce[0:1, 0:2048], self.bcast_rep[0:1, 0:2048])
        start_w = [(pe, self.p1_pe), (ve, self.p1_ve), (ac, self.p1_ac)]
        if self.debug:
            P.em("s", lambda s: s.dma_start(out=self.dbg["dbg_gfm"],
                                            in_=self.gatedfm[:]),
                 waits=list(start_w), inc=self.sout, k=16)
        ev_done = [0, 0]
        sp_done = [0, 0]
        P.em("g", lambda g: g.dma_start(out=obeta_bf[:], in_=d["out_b"]),
             waits=list(start_w), inc=sgb, k=16)
        for half in range(2):
            wld_list = []
            for c in range(C):
                ww = []
                if half == 1 and c == 0:
                    ww = [(pe, pe.n)]
                P.em("g", lambda g, c=c, half=half: g.dma_start(
                    out=o_half[:, c * 1024:(c + 1) * 1024],
                    in_=d["o_wT"][c * 128:(c + 1) * 128,
                                  half * 1024:(half + 1) * 1024]),
                    waits=ww, inc=sgb, k=16)
                wld_list.append(sgb.n)
            wld_all = sgb.n
            for m in range(T):
                ww = [(sgb, wld_all)]
                if half == 0 and m == 0:
                    ww.append((ve, self.bcast_done_ve))
                if m >= 2:
                    ww.append((ac, ev_done[m % 2]))
                for gi in range(2):
                    g0 = (m % 2) * 1024 + gi * 512
                    for c in range(C):
                        P.em("t", lambda t_, c=c, m=m, gi=gi, g0=g0, half=half: t_.matmul(
                            self.ps_s[:, g0:g0 + 512],
                            self.gatedfm[:, c * 2048 + m * 128:c * 2048 + (m + 1) * 128],
                            o_half[:, c * 1024 + gi * 512:c * 1024 + gi * 512 + 512],
                            start=(c == 0), stop=False),
                            waits=ww if (gi == 0 and c == 0) else [], inc=pe)
                    P.em("t", lambda t_, gi=gi, g0=g0, half=half: t_.matmul(
                        self.ps_s[:, g0:g0 + 512], self.ones_rb[0:1, :],
                        obeta_bf[0:1, half * 1024 + gi * 512:
                                 half * 1024 + gi * 512 + 512],
                        start=False, stop=True), inc=pe)
                wmm = pe.n
                ww2 = [(pe, wmm)]
                if m >= 2:
                    ww2.append((ssp[m % 2], sp_done[m % 2]))
                P.em("sc", lambda sc, m=m: sc.copy(
                    bounce[:, (m % 2) * 1024:(m % 2 + 1) * 1024],
                    self.ps_s[:, (m % 2) * 1024:(m % 2 + 1) * 1024]),
                    waits=ww2, inc=ac)
                ev_done[m % 2] = ac.n
                wcp = ac.n
                P.em("s", lambda s, m=m, half=half: s.dma_start(
                    out=self.attn_sp[m * 128:(m + 1) * 128,
                                     half * 1024:(half + 1) * 1024],
                    in_=bounce[:, (m % 2) * 1024:(m % 2 + 1) * 1024]),
                    waits=[(ac, wcp)], inc=ssp[m % 2], k=16)
                sp_done[m % 2] = ssp[m % 2].n
        self.p2_pe, self.p2_ac = pe.n, ac.n
        self.p2_sp = [ssp[0].n, ssp[1].n]

    def ffn_epilogue(self):
        P, d = self.P, self.d
        pe, ac, ve = self.pe, self.ac, self.ve
        sgb, sxp, sout, sconP3 = self.sgb, self.sxp, self.sout, self.sconP3
        self.ctxP2.close()
        self.ctxAB.close()
        self.ctxTP.close()
        self.ctxAB2 = ExitStack()
        ctx = ExitStack()
        self.ctxP3 = ctx
        grep = self.sb(ctx, "grep", [128, D], F32)
        gwf_sb = self.sb(ctx, "gwf_sb", [128, C * 2048], BF16)
        gbf_bf = self.sb(ctx, "gbf_bf", [1, D], BF16)
        gate_sb = self.sb(ctx, "gate_sb", [128, D], F32)
        ubuf = self.sb(ctx, "ubuf", [128, 2 * D], F32)
        xd_st = self.sb(ctx, "xd_st", [128, 2 * D], F32)
        nmu2 = self.sb(ctx, "nmu2", [128, T], F32)
        va2 = self.sb(ctx, "va2", [128, T], F32)
        rstd2 = self.sb(ctx, "rstd2", [128, T], F32)
        nmr2 = self.sb(ctx, "nmr2", [128, T], F32)
        s2 = self.sb(ctx, "s2", [128, T], F32)
        qt = self.sb(ctx, "qt", [128, 2], F32)
        kmag = self.sb(ctx, "kmag", [128, 1], I32)

        start_w = [(pe, self.p2_pe), (ac, self.p2_ac),
                   (self.ssp[0], self.p2_sp[0]), (self.ssp[1], self.p2_sp[1]),
                   (ve, ve.n)]
        P.em("s", lambda s: s.dma_start(out=ubuf[0:1, 0:D], in_=d["out_g"]),
             waits=start_w, inc=sconP3, k=16)
        wld = sconP3.n
        for gi in range(4):
            P.em("t", lambda t_, gi=gi: t_.matmul(
                self.ps_s[:, gi * 512:(gi + 1) * 512], self.ones_rf[0:1, :],
                ubuf[0:1, gi * 512:(gi + 1) * 512], start=True, stop=True),
                waits=([(sconP3, wld)] + start_w) if gi == 0 else [], inc=pe)
        w1 = pe.n
        P.em("v", lambda v: v.tensor_copy(grep[:], self.ps_s[:, 0:2048]),
             waits=[(pe, w1)], inc=ve)
        wrep = ve.n

        P.em("g", lambda g: g.dma_start(out=gbf_bf[:], in_=d["gbf"]),
             waits=[(pe, w1), (ac, self.p2_ac),
                    (self.ssp[0], self.p2_sp[0]), (self.ssp[1], self.p2_sp[1])],
             inc=sgb, k=16)
        wgwf_c = []
        for c in range(C):
            P.em("g", lambda g, c=c: g.dma_start(
                out=gwf_sb[:, c * 2048:(c + 1) * 2048],
                in_=d["gwf"][c * 128:(c + 1) * 128, :]), inc=sgb, k=16)
            wgwf_c.append(sgb.n)
        wgwf = sgb.n

        P.em("v", lambda v: v.memset(kmag[:], QMAGIC), waits=[(ve, wrep)])

        sig_last = 0
        u_ve = [0] * T
        atn_done = [0] * T
        xn_ac = [0] * T
        sq_ac = [0] * T
        quake_ve = [0] * T
        fin_ac = [0] * T
        outadd_ve = [0] * T
        outd = [0, 0]
        xld = [0, 0]

        def finish_act(mp):
            sl = mp % 2
            ub = ubuf[:, sl * D:(sl + 1) * D]
            P.em("sc", lambda sc, ub=ub, mp=mp: sc.activation(
                ub, ub, AF.Identity,
                bias=nmr2[:, mp:mp + 1], scale=rstd2[:, mp:mp + 1]),
                waits=[(ve, quake_ve[mp])], inc=ac)
            fin_ac[mp] = ac.n

        def finish_dve(mp):
            sl = mp % 2
            ub = ubuf[:, sl * D:(sl + 1) * D]
            P.em("v", lambda v, ub=ub: v.tensor_tensor(ub, ub, grep[:],
                                                       ALU.mult),
                 waits=[(ac, fin_ac[mp]), (self.sgt[sl], atn_done[mp])], inc=ve)
            P.em("v", lambda v, ub=ub, sl=sl: v.tensor_tensor(
                ub, ub, xd_st[:, sl * D:(sl + 1) * D], ALU.add),
                inc=ve, nofence=True)
            outadd_ve[mp] = ve.n
            P.em("s", lambda s, mp=mp, ub=ub: s.dma_start(
                out=self.out[mp * 128:(mp + 1) * 128, :], in_=ub),
                waits=[(ve, outadd_ve[mp])], inc=sout, k=16)
            outd[mp % 2] = sout.n

        for m in range(T):
            sl = m % 2
            wwx = [(ve, wrep)] if m < 2 else [(ve, outadd_ve[m - 2])]
            P.em("s", lambda s, m=m, sl=sl: s.dma_start(
                out=xd_st[:, sl * D:(sl + 1) * D],
                in_=d["x"][m * 128:(m + 1) * 128, :]), waits=wwx,
                inc=sxp[sl], k=16)
            xld[sl] = sxp[sl].n
            if m == 0:
                for c in range(C):
                    for gi in range(4):
                        g0 = gi * 512
                        ww = [(sgb, wgwf_c[c])] if gi == 0 else []
                        if c == 0 and gi == 0:
                            ww += [(ac, sig_last), (ve, wrep)]
                        P.em("t", lambda t_, c=c, g0=g0: t_.matmul(
                            self.ps_s[:, g0:g0 + 512],
                            self.xnfm[:, c * 2048:c * 2048 + 128],
                            gwf_sb[:, c * 2048 + g0:c * 2048 + g0 + 512],
                            start=(c == 0), stop=False), waits=ww, inc=pe)
                for gi in range(4):
                    g0 = gi * 512
                    P.em("t", lambda t_, g0=g0: t_.matmul(
                        self.ps_s[:, g0:g0 + 512], self.ones_rb[0:1, :],
                        gbf_bf[0:1, g0:g0 + 512], start=False, stop=True),
                        inc=pe)
            else:
                for gi in range(4):
                    g0 = gi * 512
                    ww = []
                    if gi == 0:
                        ww = [(sgb, wgwf), (ac, sig_last)]
                    for c in range(C):
                        P.em("t", lambda t_, c=c, m=m, g0=g0: t_.matmul(
                            self.ps_s[:, g0:g0 + 512],
                            self.xnfm[:, c * 2048 + m * 128:c * 2048 + (m + 1) * 128],
                            gwf_sb[:, c * 2048 + g0:c * 2048 + g0 + 512],
                            start=(c == 0), stop=False),
                            waits=ww if c == 0 else [], inc=pe)
                        ww = []
                    P.em("t", lambda t_, g0=g0: t_.matmul(
                        self.ps_s[:, g0:g0 + 512], self.ones_rb[0:1, :],
                        gbf_bf[0:1, g0:g0 + 512], start=False, stop=True), inc=pe)
            wmm = pe.n
            wws = [(pe, wmm)]
            if m >= 1:
                wws.append((ve, u_ve[m - 1]))
            P.em("sc", lambda sc: sc.activation(gate_sb[:], self.ps_s[:, 0:2048],
                                                AF.Sigmoid), waits=wws, inc=ac)
            wsg_ = ac.n
            sig_last = ac.n
            wwn = [(sxp[sl], xld[sl])]
            if m >= 2:
                wwn.append((sout, outd[sl]))
            P.em("sc", lambda sc, m=m, sl=sl: sc.activation(
                ubuf[:, sl * D:(sl + 1) * D],
                xd_st[:, sl * D:(sl + 1) * D], AF.Identity,
                bias=self.nmr_st[:, m:m + 1], scale=self.rstd_st[:, m:m + 1]),
                waits=wwn, inc=ac)
            xn_ac[m] = ac.n
            satn = self.sgt[sl]
            P.em("g", lambda g, m=m, sl=sl: g.dma_start(
                out=xd_st[:, sl * D:(sl + 1) * D],
                in_=self.attn_sp[m * 128:(m + 1) * 128, :],
                accum_op=ALU.add),
                waits=[(sxp[sl], xld[sl]), (ac, xn_ac[m])], inc=satn, k=16)
            atn_done[m] = satn.n
            if m >= 1:
                finish_act(m - 1)
            P.em("v", lambda v: v.tensor_tensor(gate_sb[:], gate_sb[:],
                                                self.bcast_rep[:], ALU.mult),
                 waits=[(ac, wsg_)], inc=ve)
            P.em("v", lambda v, sl=sl: v.tensor_tensor(
                ubuf[:, sl * D:(sl + 1) * D],
                ubuf[:, sl * D:(sl + 1) * D], gate_sb[:], ALU.add),
                waits=[(ac, xn_ac[m])], inc=ve)
            u_ve[m] = ve.n
            if m >= 1:
                finish_dve(m - 1)
            P.em("sc", lambda sc, m=m, sl=sl: sc.activation(
                gate_sb[:], ubuf[:, sl * D:(sl + 1) * D], AF.Identity,
                accum_out=s2[:, m:m + 1]), waits=[(ve, u_ve[m])], inc=ac)
            wus = ac.n
            P.em("v", lambda v, m=m: v.tensor_scalar_mul(
                nmu2[:, m:m + 1], s2[:, m:m + 1], -1.0 / D),
                waits=[(ac, wus)])
            wng = ve.n
            P.em("sc", lambda sc, m=m, sl=sl: sc.activation(
                gate_sb[:], ubuf[:, sl * D:(sl + 1) * D], AF.Square,
                bias=nmu2[:, m:m + 1], accum_out=va2[:, m:m + 1]),
                waits=[(ve, wng)], inc=ac)
            sq_ac[m] = ac.n
            yc = rstd2[:, m:m + 1]
            yi = rstd2.bitcast(I32)[:, m:m + 1]
            tc = qt[:, 1:2]
            P.em("v", lambda v, m=m: v.tensor_scalar(
                qt[:, 0:1], va2[:, m:m + 1], 1.0 / D, EPS2, ALU.mult, ALU.add),
                waits=[(ac, sq_ac[m])])
            P.em("v", lambda v, yi=yi: v.tensor_scalar(
                yi, qt.bitcast(I32)[:, 0:1], 1, None, ALU.logical_shift_right))
            P.em("v", lambda v, yi=yi: v.tensor_tensor(
                yi, kmag[:], yi, ALU.subtract))
            for _ in range(3):
                P.em("v", lambda v, yc=yc, tc=tc: v.tensor_tensor(
                    tc, qt[:, 0:1], yc, ALU.mult))
                P.em("v", lambda v, yc=yc, tc=tc: v.tensor_tensor(
                    tc, tc, yc, ALU.mult))
                P.em("v", lambda v, tc=tc: v.tensor_scalar(
                    tc, tc, -0.5, 1.5, ALU.mult, ALU.add))
                P.em("v", lambda v, yc=yc, tc=tc: v.tensor_tensor(
                    yc, yc, tc, ALU.mult))
            P.em("v", lambda v, m=m, yc=yc: v.tensor_tensor(
                nmr2[:, m:m + 1], nmu2[:, m:m + 1], yc, ALU.mult))
            quake_ve[m] = ve.n

        finish_act(T - 1)
        finish_dve(T - 1)
        if self.debug:
            P.em("s", lambda s: s.dma_start(out=self.dbg["dbg_attn"],
                                            in_=self.attn_sp),
                 waits=[(self.ssp[0], self.ssp[0].n),
                        (self.ssp[1], self.ssp[1].n)], inc=self.sout, k=16)
        P.em("s", lambda s: s.nop(), waits=[(sout, sout.n)])

    def build(self):
        self.declare_io()
        self.alloc_outer()
        self.load_consts()
        self.thought_alloc()
        self.phase0()
        self.thought_prologue()
        self.pre_attn_pe = self.pe.n
        self.pre_attn_ve = self.ve.n
        self.pre_attn_ac = self.ac.n
        if self.debug:
            P, ve, sout = self.P, self.ve, self.sout
            P.em("s", lambda s: s.dma_start(out=self.dbg["dbg_xnfm"],
                                            in_=self.xnfm[:]),
                 waits=[(ve, ve.n)], inc=sout, k=16)
        self.P.flush(self.nc, "b0")
        self.ctxT.close()
        self.ctxP0.close()
        self.attention()
        self.P.flush(self.nc, "b1")
        self.oproj()
        self.P.flush(self.nc, "b2")
        self.ffn_epilogue()
        self.P.flush(self.nc, "b3")
        self.ctxP3.close()
        self.ctxAB2.close()
        self.ctx0.close()
        return self.nc


def _host_layout(inputs):
    f = np.float32

    def a(x):
        return np.ascontiguousarray(np.asarray(x, dtype=f))

    x_all = a(inputs["L0_attn_x"])
    qw = a(inputs["L0_attn_q_w"])
    kw = a(inputs["L0_attn_k_w"])
    vw = a(inputs["L0_attn_v_w"])
    ow = a(inputs["L0_attn_o_w"])

    def pack(w):
        return np.ascontiguousarray(
            w.T.reshape(C, 128, H, 128).transpose(1, 2, 0, 3))

    common = {
        "wq_p": pack(qw), "wk_p": pack(kw), "wv_p": pack(vw),
        "o_wT": a(ow.T),
        "gw_aT": a(inputs["L0_attn_gate_w"].T),
        "gb_a": a(inputs["L0_attn_gate_b"]).reshape(1, HD),
        "identity": np.eye(128, dtype=f),
        "ones_row": np.ones((1, 128), f),
        "ones_col": np.ones((128, 1), f),
        "nsl": np.triu(np.full((128, 128), -1e30, f), 1),
        "agg_w": a(inputs["L0_ffn_agg_w"]),
        "d1_w": a(inputs["L0_ffn_d1_w"]),
        "d2_w": a(inputs["L0_ffn_d2_w"]),
        "bc_w": a(inputs["L0_ffn_bc_w"]),
        "gwf": a(inputs["L0_ffn_gate_w"]),
        "gbf": a(inputs["L0_ffn_gate_b"]).reshape(1, D),
        "agg_bT": np.ascontiguousarray(
            a(inputs["L0_ffn_agg_b"]).reshape(C, 128).T),
        "d1bT": np.ascontiguousarray(
            a(inputs["L0_ffn_d1_b"]).reshape(64, 128).T),
        "d2bT": np.ascontiguousarray(
            a(inputs["L0_ffn_d2_b"]).reshape(C, 128).T),
        "bcb": a(inputs["L0_ffn_bc_b"]).reshape(1, D),
        "in_g": a(inputs["L0_ffn_in_gamma"]).reshape(1, D),
        "in_b": a(inputs["L0_ffn_in_beta"]).reshape(1, D),
        "pn_gT": np.ascontiguousarray(
            a(inputs["L0_ffn_pn_gamma"]).reshape(C, 128).T),
        "pn_bT": np.ascontiguousarray(
            a(inputs["L0_ffn_pn_beta"]).reshape(C, 128).T),
        "out_g": a(inputs["L0_ffn_out_gamma"]).reshape(1, D),
        "out_b": a(inputs["L0_ffn_out_beta"]).reshape(1, D),
    }
    in_maps = []
    for b in range(8):
        m = dict(common)
        m["x"] = np.ascontiguousarray(x_all[b])
        in_maps.append(m)
    return in_maps


def kernel(**inputs):
    in_maps = _host_layout(inputs)
    nc = Builder().build()
    res = run_bass_kernel_spmd(nc, in_maps, core_ids=list(range(8)))
    return np.stack([np.asarray(r["out"], np.float32) for r in res.results], axis=0)


if __name__ == "__main__":
    nc = Builder().build()
    print("built ok; instructions:", nc.n_instructions())
